# version 10
# baseline (speedup 1.0000x reference)
import numpy as np
import ml_dtypes
import concourse.bass as bass
import concourse.mybir as mybir
from concourse.bass_utils import run_bass_kernel_spmd
from contextlib import ExitStack

F32 = mybir.dt.float32
BF16 = mybir.dt.bfloat16
AF = mybir.ActivationFunctionType
OP = mybir.AluOpType
AX = mybir.AxisListType

D = 1024
DFF = 4096
NE = 8
SEQ = 8192
B = 2
TOWN = 2048
L0 = 128
LB = 2304
NT = LB // 128
NTO = TOWN // 128
NC = 8
EPS = 1e-5

TT_FULL = [(0, 512), (512, 512), (1024, 512), (1536, 512), (2048, 256)]
TT_OWN = [(0, 512), (512, 512), (1024, 512), (1536, 512)]

ENGS = ("sync", "gpsimd", "scalar", "vector", "tensor")

DBG = {}
DBG_SRC = {}
STAGE_LIMIT = 99


class Sem:
    def __init__(self, nc, es, name):
        self.h = es.enter_context(nc.semaphore(name))
        self.n = 0


class Prog:

    def __init__(self):
        self.streams = {e: [] for e in ENGS}

    def emit(self, eng, fn):
        self.streams[eng].append(fn)

    def wait(self, eng, sem, val):
        h = sem.h
        self.emit(eng, lambda e: e.wait_ge(h, val))

    def inc(self, eng, sem, buildf, k=1):
        sem.n += k
        h = sem.h
        self.emit(eng, lambda e: buildf(e).then_inc(h, k))

    def dma(self, eng, sem, out, in_):
        sem.n += 16
        h = sem.h
        self.emit(eng, lambda e: e.dma_start(out=out, in_=in_).then_inc(h, 16))

    def run(self, block):
        for eng in ENGS:
            thunks = self.streams[eng]
            def body(e, thunks=thunks):
                for f in thunks:
                    f(e)
            getattr(block, eng)(body)



def _finish(nc, P, block_debug):
    d_dbg, loc = block_debug
    if d_dbg:
        ctx = {k: loc[k] for k in ("XF", "AQ", "AK", "AV", "H2", "glog", "gw",
                                   "v_ddf", "v_ddtok", "v_taptok", "v_tapf",
                                   "STAT", "sbc", "H2f", "WS")
               if k in loc}
        s_ve = loc["s_ve"]; s_sc = loc["s_sc"]; s_pe = loc["s_pe"]
        s_st = loc["s_st"][0]
        P.wait("sync", s_ve, s_ve.n)
        P.wait("sync", s_sc, s_sc.n)
        P.wait("sync", s_pe, s_pe.n)
        for name, ap in d_dbg.items():
            P.dma("sync", s_st, ap, DBG_SRC[name](ctx))
        P.wait("sync", s_st, s_st.n)
    with nc.allow_low_precision("bf16 accumulators (see build_program)"):
        with nc.Block() as block:
            P.run(block)


def build_program(debug_taps=()):
    nc = bass.Bass("TRN2", target_bir_lowering=False)
    es = ExitStack()

    d_xh = nc.dram_tensor("xh", [LB, D], F32, kind="ExternalInput").ap()
    d_wqkv = nc.dram_tensor("wqkv", [D, 3 * D], F32, kind="ExternalInput").ap()
    d_wout = nc.dram_tensor("wout", [D, D], F32, kind="ExternalInput").ap()
    d_bout = nc.dram_tensor("bout", [1, D], F32, kind="ExternalInput").ap()
    d_ctt = nc.dram_tensor("ctt", [D, NE], F32, kind="ExternalInput").ap()
    d_w1 = nc.dram_tensor("w1", [NE, D, DFF], F32, kind="ExternalInput").ap()
    d_b1t = nc.dram_tensor("b1t", [128, NE * 32], F32, kind="ExternalInput").ap()
    d_w2 = nc.dram_tensor("w2", [NE, DFF, D], F32, kind="ExternalInput").ap()
    d_b2 = nc.dram_tensor("b2", [NE, D], F32, kind="ExternalInput").ap()
    d_eyeb = nc.dram_tensor("eyeb", [128, 128], BF16, kind="ExternalInput").ap()
    d_eyef = nc.dram_tensor("eyef", [128, 128], F32, kind="ExternalInput").ap()
    d_bonesA = nc.dram_tensor("bonesA", [128, 256], BF16, kind="ExternalInput").ap()
    d_bonesB = nc.dram_tensor("bonesB", [128, 256], BF16, kind="ExternalInput").ap()
    d_eone = nc.dram_tensor("eone", [128, 224], BF16, kind="ExternalInput").ap()
    d_eb = nc.dram_tensor("eb", [128, 7168], BF16, kind="ExternalInput").ap()
    d_masks = nc.dram_tensor("masks", [128, 64], F32, kind="ExternalInput").ap()
    d_desc8 = nc.dram_tensor("desc8", [128, 128], F32, kind="ExternalInput").ap()
    d_countv = nc.dram_tensor("countv", [1, 2176], BF16, kind="ExternalInput").ap()
    d_out = nc.dram_tensor("out", [TOWN, D], F32, kind="ExternalOutput").ap()
    d_dbg = {}
    for name, (shape, dt) in dict(debug_taps).items():
        d_dbg[name] = nc.dram_tensor("dbg_" + name, list(shape), dt,
                                     kind="ExternalOutput").ap()

    t_XF = es.enter_context(nc.sbuf_tensor("XF", [128, 8 * LB], BF16))
    t_AQ = es.enter_context(nc.sbuf_tensor("AQ", [128, 8 * LB], BF16))
    t_AK = es.enter_context(nc.sbuf_tensor("AK", [128, 8 * LB], BF16))
    t_AV = es.enter_context(nc.sbuf_tensor("AV", [128, 8 * LB], BF16))
    t_H2 = es.enter_context(nc.sbuf_tensor("H2", [128, 14336], BF16))
    t_WS = es.enter_context(nc.sbuf_tensor("WS", [128, 10240], BF16))
    XF, AQ, AK, AV, H2, WS = (t.ap() for t in (t_XF, t_AQ, t_AK, t_AV, t_H2, t_WS))
    AVf = AV.bitcast(F32)
    H2f = H2.bitcast(F32)

    small = {}
    for nm, shape, dt in [
            ("eyeb", [128, 128], BF16), ("eyef", [128, 128], F32),
            ("bonesA", [128, 256], BF16), ("bonesB", [128, 256], BF16),
            ("eone", [128, 224], BF16),
            ("masks", [128, 64], F32), ("countv", [1, 2176], BF16),
            ("desc8", [128, 128], F32),
            ("ctf", [128, 64], F32), ("b1s", [128, NE * 32], F32),
            ("b2s", [NE, D], BF16), ("bouts", [1, D], BF16),
            ("glog", [128, 128], F32), ("gw", [128, 128], F32),
            ("sbc", [128, 8], F32), ("srow", [1, 8], F32),
            ("ones1", [1, 128], F32), ("ones128", [128, 1], F32),
            ("gT", [NE, 128], BF16), ("tmp2", [128, 512], BF16),
            ("epsv", [128, 1], F32),
            ("stat", [128, 16], F32)]:
        small[nm] = es.enter_context(nc.sbuf_tensor("sb_" + nm, shape, dt)).ap()
    eyeb, eyef = small["eyeb"], small["eyef"]
    bonesA, bonesB, eone = small["bonesA"], small["bonesB"], small["eone"]
    masks, countv, ctf, b1s = (small[x] for x in ("masks", "countv", "ctf", "b1s"))
    b2s, bouts, glog, gw = (small[x] for x in ("b2s", "bouts", "glog", "gw"))
    sbc, srow, ones1, ones128 = (small[x] for x in
                                 ("sbc", "srow", "ones1", "ones128"))
    gT, tmp2, STAT = small["gT"], small["tmp2"], small["stat"]

    v_xt = [AVf[:, 0:1024], AVf[:, 1024:2048]]
    v_sq = AVf[:, 2048:3072]
    v_xtr = [[AVf[:, 3072 + bb * 1024 + c * 128: 3072 + bb * 1024 + (c + 1) * 128]
              for c in range(8)] for bb in range(2)]
    v_xnt = [AV[:, 10240:11264], AV[:, 11264:12288]]

    v_P = [H2[:, 0:2048], H2[:, 2048:4096]]
    v_ddf = H2[:, 4608:4608 + 2048]
    v_ddtok = H2[:, 6656:6656 + 1792]
    v_taptok = H2[:, 8448:8448 + 1792]
    v_tapf = H2[:, 10240:10240 + 2048]
    v_sm = [H2[:, 12288 + i * 256: 12288 + (i + 1) * 256] for i in range(8)]
    v_woutb = [H2[:, 1024 * c: 1024 * (c + 1)] for c in range(8)]
    v_lgscr = v_sm[0].bitcast(F32)

    def h_view(fc, c0, c1):
        if fc < 18:
            return AK[:, 1024 * fc + c0: 1024 * fc + c1]
        return H2[:, 1024 * (fc - 18) + c0: 1024 * (fc - 18) + c1]

    v_xt2 = [H2f[:, 0:1024], H2f[:, 1024:2048]]
    v_sg = [H2f[:, 2048:3072], H2f[:, 3072:4096]]
    v_outb = [H2f[:, 4096:5120], H2f[:, 5120:6144], H2f[:, 6144:7168]]
    v_tmpf = H2f[:, 3072:4096]


    def wgrp(bb, dc):
        return WS[:, 4096 * bb + 512 * dc: 4096 * bb + 512 * (dc + 1)]
    v_w2buf = [WS[:, 8192 + 512 * i: 8192 + 512 * (i + 1)] for i in range(4)]

    PB = [es.enter_context(nc.psum_tensor("pb%d" % i, [128, 512], F32)).ap()
          for i in range(8)]
    PBh = [p.bitcast(BF16) for p in PB]

    SL = STAGE_LIMIT
    P = Prog()
    s_x = [Sem(nc, es, "s_x0"), Sem(nc, es, "s_x1")]
    s_c = Sem(nc, es, "s_c")
    s_wq = [Sem(nc, es, "s_wq0"), Sem(nc, es, "s_wq1")]
    s_wo = Sem(nc, es, "s_wo")
    s_eb = Sem(nc, es, "s_eb")
    s_w1 = [Sem(nc, es, "s_w1a"), Sem(nc, es, "s_w1b")]
    s_w2 = [Sem(nc, es, "s_w2%d" % i) for i in range(4)]
    s_sc = Sem(nc, es, "s_sc")
    s_ve = Sem(nc, es, "s_ve")
    s_pe = Sem(nc, es, "s_pe")
    s_st = [Sem(nc, es, "s_st%d" % i) for i in range(3)]
    s_x2 = [Sem(nc, es, "s_x2a"), Sem(nc, es, "s_x2b")]

    def fence(engs=("tensor", "vector", "scalar", "sync")):
        vals = [(s_pe, s_pe.n), (s_ve, s_ve.n), (s_sc, s_sc.n)]
        for eng in engs:
            for sem, v in vals:
                P.wait(eng, sem, v)

    s_vb = Sem(nc, es, "s_vb")
    vbs = small["stat"][:, 14:16]

    def vbar():
        P.inc("vector", s_vb, lambda e: e.tensor_copy(vbs[0:1, 0:1],
                                                      vbs[0:1, 1:2]))
        P.wait("vector", s_vb, s_vb.n)

    P.dma("gpsimd", s_c, small["eyeb"][:], d_eyeb[:])
    P.dma("gpsimd", s_c, small["eyef"][:], d_eyef[:])
    P.dma("gpsimd", s_c, bonesA[:], d_bonesA[:])
    P.dma("gpsimd", s_c, bonesB[:], d_bonesB[:])
    P.dma("gpsimd", s_c, eone[:], d_eone[:])
    P.dma("gpsimd", s_c, masks[:], d_masks[:])
    P.dma("gpsimd", s_c, small["desc8"][:], d_desc8[:])
    P.dma("gpsimd", s_c, countv[:], d_countv[:])
    P.dma("gpsimd", s_c, b1s[:], d_b1t[:])
    P.dma("gpsimd", s_c, b2s[:], d_b2[:])
    P.dma("gpsimd", s_c, bouts[:], d_bout[:])
    for dc in range(8):
        P.dma("gpsimd", s_c, ctf[:, 8 * dc:8 * dc + 8],
              d_ctt[128 * dc:128 * (dc + 1), :])
    NCONST = s_c.n

    P.emit("vector", lambda e: e.memset(ones1[:], 1.0))
    P.emit("vector", lambda e: e.memset(small["epsv"][:], EPS))
    P.inc("vector", s_ve, lambda e: e.memset(ones128[:], 1.0))
    VE_ONES = s_ve.n

    P.wait("tensor", s_c, NCONST)
    P.wait("tensor", s_ve, VE_ONES)

    def mm_srow(e):
        for dc in range(8):
            i = e.matmul(PB[6][0:1, 0:8], ones128[:], ctf[:, 8 * dc:8 * dc + 8],
                         start=(dc == 0), stop=(dc == 7))
        return i
    P.inc("tensor", s_pe, mm_srow)
    P.wait("vector", s_pe, s_pe.n)
    P.inc("vector", s_ve, lambda e: e.tensor_copy(srow[:], PB[6][0:1, 0:8]))
    P.wait("tensor", s_ve, s_ve.n)
    P.inc("tensor", s_pe, lambda e: e.matmul(PB[7][:, 0:8], ones1[:], srow[:],
                                             start=True, stop=True))
    P.wait("vector", s_pe, s_pe.n)
    P.inc("vector", s_ve, lambda e: e.tensor_copy(sbc[:], PB[7][:, 0:8]))
    VE_SBC = s_ve.n

    A_xn, A_bt, A_gmm, A_glog = {}, {}, {}, {}
    A_bt_ev, A_ft_ev = {}, {}
    for k in range(NT):
        bb = k % 2
        owned = 1 <= k <= NTO
        if k >= 2:
            P.wait("sync", s_ve, A_xn[k - 2])
            P.wait("sync", s_pe, A_gmm.get(k - 2, A_bt.get(k - 2, 0)))
        P.dma("sync", s_x[bb], v_xt[bb][:], d_xh[128 * k:128 * (k + 1), :])
        XLD = s_x[bb].n

        P.wait("scalar", s_x[bb], XLD)
        P.emit("scalar", lambda e, bb=bb: e.activation(
            out=v_sq[:], in_=v_xt[bb][:], func=AF.Square,
            accum_out=STAT[:, 2 + bb:3 + bb]))
        P.inc("scalar", s_sc, lambda e, bb=bb: e.activation(
            out=v_sq[:], in_=v_xt[bb][:], func=AF.Copy,
            accum_out=STAT[:, 0 + bb:1 + bb]))
        ASTAT = s_sc.n

        P.wait("vector", s_sc, ASTAT)
        P.emit("vector", lambda e, bb=bb: e.tensor_scalar(
            out=STAT[:, 6 + bb:7 + bb], in0=STAT[:, 2 + bb:3 + bb],
            scalar1=1.0 / D, scalar2=None, op0=OP.mult))
        P.emit("vector", lambda e, bb=bb: e.tensor_scalar(
            out=STAT[:, 4 + bb:5 + bb], in0=STAT[:, 0 + bb:1 + bb],
            scalar1=1.0 / D, scalar2=None, op0=OP.mult))
        vbar()
        P.inc("vector", s_ve, lambda e, bb=bb: e.scalar_tensor_tensor(
            out=STAT[:, 6 + bb:7 + bb], in0=STAT[:, 4 + bb:5 + bb],
            scalar=STAT[:, 4 + bb:5 + bb], in1=STAT[:, 6 + bb:7 + bb],
            op0=OP.mult, op1=OP.subtract))
        AVAR = s_ve.n
        P.wait("scalar", s_ve, AVAR)
        P.inc("scalar", s_sc, lambda e, bb=bb: e.activation(
            out=STAT[:, 8 + bb:9 + bb], in_=STAT[:, 6 + bb:7 + bb],
            func=AF.Sqrt, bias=small["epsv"][:], scale=-1.0))
        ASD = s_sc.n
        P.wait("vector", s_sc, ASD)
        P.emit("vector", lambda e, bb=bb: e.reciprocal(
            STAT[:, 10 + bb:11 + bb], STAT[:, 8 + bb:9 + bb]))
        vbar()
        P.inc("vector", s_ve, lambda e, bb=bb: e.tensor_scalar(
            out=v_xnt[bb][:], in0=v_xt[bb][:],
            scalar1=STAT[:, 4 + bb:5 + bb], scalar2=STAT[:, 10 + bb:11 + bb],
            op0=OP.subtract, op1=OP.mult))
        A_xn[k] = s_ve.n

        P.wait("tensor", s_ve, A_xn[k])
        for c in range(8):
            pbi = 4 + (c % 2)
            prev = A_bt_ev.get((k, c - 2), A_bt_ev.get((k - 1, 6 + (c % 2))))
            if prev:
                P.wait("tensor", s_ve, prev)
            P.inc("tensor", s_pe, lambda e, bb=bb, c=c, pbi=pbi: e.transpose(
                PBh[pbi][:, 0:128], v_xnt[bb][:, 128 * c:128 * (c + 1)], eyeb[:]))
            A_bt[k] = s_pe.n
            P.wait("vector", s_pe, A_bt[k])
            P.inc("vector", s_ve, lambda e, k=k, c=c, pbi=pbi: e.tensor_copy(
                XF[:, LB * c + 128 * k: LB * c + 128 * (k + 1)],
                PBh[pbi][:, 0:128]))
            A_bt_ev[(k, c)] = s_ve.n
        if owned:
            for c in range(8):
                pbi = 2 + (c % 2)
                prev = A_ft_ev.get((k, c - 2), A_ft_ev.get((k - 1, 6 + (c % 2))))
                if prev:
                    P.wait("tensor", s_ve, prev)
                P.inc("tensor", s_pe, lambda e, bb=bb, c=c, pbi=pbi: e.transpose(
                    PB[pbi][:, 0:128], v_xt[bb][:, 128 * c:128 * (c + 1)], eyef[:]))
                AFT = s_pe.n
                P.wait("vector", s_pe, AFT)
                P.inc("vector", s_ve, lambda e, bb=bb, c=c, pbi=pbi: e.tensor_copy(
                    v_xtr[bb][c][:], PB[pbi][:, 0:128]))
                A_ft_ev[(k, c)] = s_ve.n
            gb = (k - 1) % 2
            P.wait("tensor", s_ve, A_ft_ev[(k, 7)])
            if k >= 3:
                P.wait("tensor", s_ve, A_glog[k - 2])

            def gmm(e, bb=bb, gb=gb):
                for dc in range(8):
                    i = e.matmul(PB[gb][:, 0:8], v_xtr[bb][dc][:],
                                 ctf[:, 8 * dc:8 * dc + 8],
                                 start=(dc == 0), stop=(dc == 7))
                return i
            P.inc("tensor", s_pe, gmm)
            A_gmm[k] = s_pe.n
            P.wait("vector", s_pe, A_gmm[k])
            if k == 1:
                P.wait("vector", s_ve, VE_SBC)
            P.emit("vector", lambda e, bb=bb, gb=gb: e.scalar_tensor_tensor(
                out=v_lgscr[:, 0:8], in0=sbc[:],
                scalar=STAT[:, 4 + bb:5 + bb], in1=PB[gb][:, 0:8],
                op0=OP.mult, op1=OP.subtract))
            vbar()
            P.inc("vector", s_ve, lambda e, k=k, bb=bb: e.tensor_scalar(
                out=glog[:, 8 * (k - 1):8 * k], in0=v_lgscr[:, 0:8],
                scalar1=STAT[:, 10 + bb:11 + bb], scalar2=-1.0,
                op0=OP.mult, op1=OP.mult))
            A_glog[k] = s_ve.n

    P.wait("scalar", s_ve, A_glog[NTO])
    P.inc("scalar", s_sc, lambda e: e.activation(out=v_lgscr[:, 0:128],
                                                 in_=glog[:], func=AF.Exp,
                                                 scale=-1.0))
    SIG = s_sc.n
    P.wait("vector", s_sc, SIG)
    P.emit("vector", lambda e: e.tensor_scalar(
        out=v_lgscr[:, 0:128], in0=v_lgscr[:, 0:128], scalar1=1.0,
        scalar2=None, op0=OP.add))
    P.emit("vector", lambda e: e.reciprocal(glog[:], v_lgscr[:, 0:128]))
    gv = glog.rearrange("p (t e) -> p t e", e=8)
    gwv = gw.rearrange("p (t e) -> p t e", e=8)
    mmf = v_sm[1].bitcast(F32)
    m1 = mmf[:, 0:16]
    t1 = mmf[:, 16:32]
    m2v = mmf[:, 32:48]
    t2 = mmf[:, 48:64]
    den = mmf[:, 64:80]
    rden = mmf[:, 80:96]
    scr = v_sm[2].bitcast(F32)[:, 0:128].rearrange("p (t e) -> p t e", e=8)
    scr2 = v_sm[3].bitcast(F32)[:, 0:128].rearrange("p (t e) -> p t e", e=8)
    descv = small["desc8"].rearrange("p (t e) -> p t e", e=8)

    def bc16(ap):
        return ap.unsqueeze(2).broadcast_to([128, 16, 8])

    P.emit("vector", lambda e: e.tensor_reduce(out=m1[:], in_=gv, axis=AX.X,
                                               op=OP.max))
    vbar()
    P.emit("vector", lambda e: e.tensor_tensor(out=scr, in0=gv, in1=bc16(m1),
                                               op=OP.is_equal))
    P.emit("vector", lambda e: e.tensor_tensor(out=scr, in0=scr, in1=descv,
                                               op=OP.mult))
    vbar()
    P.emit("vector", lambda e: e.tensor_reduce(out=t1[:], in_=scr, axis=AX.X,
                                               op=OP.max))
    vbar()
    P.emit("vector", lambda e: e.tensor_tensor(out=scr, in0=scr, in1=bc16(t1),
                                               op=OP.is_equal))
    vbar()
    P.emit("vector", lambda e: e.scalar_tensor_tensor(
        out=scr2, in0=scr, scalar=-1e30, in1=gv, op0=OP.mult, op1=OP.add))
    vbar()
    P.emit("vector", lambda e: e.tensor_reduce(out=m2v[:], in_=scr2, axis=AX.X,
                                               op=OP.max))
    vbar()
    P.emit("vector", lambda e: e.tensor_tensor(out=scr2, in0=scr2, in1=bc16(m2v),
                                               op=OP.is_equal))
    P.emit("vector", lambda e: e.tensor_tensor(out=scr2, in0=scr2, in1=descv,
                                               op=OP.mult))
    vbar()
    P.emit("vector", lambda e: e.tensor_reduce(out=t2[:], in_=scr2, axis=AX.X,
                                               op=OP.max))
    vbar()
    P.emit("vector", lambda e: e.tensor_tensor(out=scr2, in0=scr2, in1=bc16(t2),
                                               op=OP.is_equal))
    vbar()
    P.emit("vector", lambda e: e.tensor_tensor(out=scr, in0=scr, in1=scr2,
                                               op=OP.add))
    P.emit("vector", lambda e: e.tensor_tensor(out=den[:], in0=m1[:], in1=m2v[:],
                                               op=OP.add))
    vbar()
    P.emit("vector", lambda e: e.reciprocal(rden[:], den[:]))
    vbar()
    P.emit("vector", lambda e: e.tensor_tensor(out=gwv, in0=gv, in1=scr,
                                               op=OP.mult))
    P.inc("vector", s_ve, lambda e: e.tensor_tensor(
        out=gwv, in0=gwv, in1=bc16(rden), op=OP.mult))
    GATES = s_ve.n

    fence()

    if SL < 2:
        _finish(nc, P, block_debug=(d_dbg, locals()))
        return nc
    C_grp_pe = {}
    C_ev = []
    ngC = 0
    for jg in range(6):
        if jg >= 2:
            P.wait("gpsimd", s_pe, C_grp_pe[jg - 2])
        for dc in range(8):
            P.dma("gpsimd", s_wq[jg % 2], wgrp(jg % 2, dc),
                  d_wqkv[128 * dc:128 * (dc + 1), 512 * jg:512 * (jg + 1)])
        WQL = s_wq[jg % 2].n

        P.wait("tensor", s_wq[jg % 2], WQL)
        for jci in range(4):
            j = 512 * jg + 128 * jci
            for (toff, tn) in TT_FULL:
                pbi = ngC % 4
                if ngC >= 4:
                    P.wait("tensor", s_ve, C_ev[ngC - 4])
                ngC += 1

                def qmm(e, jg=jg, jci=jci, toff=toff, tn=tn, pbi=pbi):
                    for dc in range(8):
                        i = e.matmul(
                            PB[pbi][:, 0:tn],
                            wgrp(jg % 2, dc)[:, 128 * jci:128 * (jci + 1)],
                            XF[:, LB * dc + toff: LB * dc + toff + tn],
                            start=(dc == 0), stop=(dc == 7))
                    return i
                P.inc("tensor", s_pe, qmm)
                MM = s_pe.n
                C_grp_pe[jg] = MM
                c = (j % 1024) // 128
                dst = (AQ, AK, AV)[j // 1024]
                P.wait("vector", s_pe, MM)
                P.inc("vector", s_ve, lambda e, dst=dst, c=c, toff=toff, tn=tn,
                      pbi=pbi: e.tensor_copy(
                          dst[:, LB * c + toff: LB * c + toff + tn],
                          PB[pbi][:, 0:tn]))
                C_ev.append(s_ve.n)

    fence()

    if SL < 3:
        _finish(nc, P, block_debug=(d_dbg, locals()))
        return nc
    D1_pe, D1_ev = {}, {}
    for dp in range(4):
        dls = [2 * dp] + ([2 * dp + 1] if dp < 3 else [])
        for c in range(8):
            prev = D1_pe.get((dp, c - 1), D1_pe.get((dp - 1, 7)))
            if prev:
                P.wait("vector", s_pe, prev)
            for li, dlt in enumerate(dls):
                dv = dlt - 3
                P.inc("vector", s_ve, lambda e, c=c, dv=dv, li=li: e.tensor_tensor(
                    out=v_P[li][:, 0:TOWN],
                    in0=AQ[:, LB * c + L0: LB * c + L0 + TOWN],
                    in1=AK[:, LB * c + L0 + dv: LB * c + L0 + dv + TOWN],
                    op=OP.mult))
            PDONE = s_ve.n
            P.wait("tensor", s_ve, PDONE)
            if c == 0 and dp > 0:
                P.wait("tensor", s_ve, D1_ev[dp - 1])

            def ddmm(e, c=c, dls=dls):
                for tt in range(4):
                    for li, dlt in enumerate(dls):
                        bm = (bonesA, bonesB)[li]
                        i = e.matmul(PB[tt][0:32, 0:512],
                                     bm[:, 32 * c:32 * (c + 1)],
                                     v_P[li][:, 512 * tt:512 * (tt + 1)],
                                     start=(c == 0 and li == 0),
                                     stop=(c == 7 and li == len(dls) - 1))
                return i
            P.inc("tensor", s_pe, ddmm)
            D1_pe[(dp, c)] = s_pe.n
        P.wait("vector", s_pe, D1_pe[(dp, 7)])
        nrow = 16 * len(dls)
        for tt in range(4):
            P.inc("vector", s_ve, lambda e, dp=dp, tt=tt, nrow=nrow: e.tensor_copy(
                v_ddf[32 * dp:32 * dp + nrow, 512 * tt:512 * (tt + 1)],
                PB[tt][0:nrow, 0:512]))
        D1_ev[dp] = s_ve.n
    K_DONE = s_pe.n

    if SL < 4:
        _finish(nc, P, block_debug=(d_dbg, locals()))
        return nc
    P.wait("gpsimd", s_ve, s_ve.n)
    P.dma("gpsimd", s_eb, AQ[:, 0:7168], d_eb[:])
    EBL = s_eb.n

    D15_ev = {}
    for tc in range(NTO):
        pbi = 4 + tc % 2
        if tc >= 2:
            P.wait("tensor", s_ve, D15_ev[tc - 2])
        if tc < 2:
            P.wait("tensor", s_ve, D1_ev[3])
        P.inc("tensor", s_pe, lambda e, tc=tc, pbi=pbi: e.transpose(
            PBh[pbi][:, 0:128], v_ddf[0:112, 128 * tc:128 * (tc + 1)],
            eyeb[0:112, :]))
        TP = s_pe.n
        P.wait("vector", s_pe, TP)
        P.inc("vector", s_ve, lambda e, tc=tc, pbi=pbi: e.tensor_copy(
            v_ddtok[:, 112 * tc:112 * (tc + 1)], PBh[pbi][:, 0:112]))
        D15_ev[tc] = s_ve.n

    P.inc("vector", s_ve, lambda e: e.memset(v_taptok[:], 0.0))
    ddt = v_ddtok.rearrange("p (t x) -> p t x", x=112)
    tapt = v_taptok.rearrange("p (t x) -> p t x", x=112)
    ebufs = [v_sm[i].rearrange("p (t h) -> p t h", h=16) for i in range(4)]
    mbuf = v_sm[4].rearrange("p (t h) -> p t h", h=16)
    rmbuf = v_sm[5].rearrange("p (t h) -> p t h", h=16)
    d2scr = v_sm[6].rearrange("p (t h) -> p t h", h=16)

    for qi in range(4):
        dk = [ddt[:, :, 16 * (kj - qi + 3): 16 * (kj - qi + 3) + 16]
              for kj in range(4)]
        P.emit("vector", lambda e, dk=dk: e.tensor_tensor(
            out=mbuf, in0=dk[0], in1=dk[1], op=OP.max))
        P.emit("vector", lambda e, dk=dk: e.tensor_tensor(
            out=d2scr, in0=dk[2], in1=dk[3], op=OP.max))
        P.emit("vector", lambda e: e.tensor_tensor(
            out=mbuf, in0=mbuf, in1=d2scr, op=OP.max))
        for kj in range(4):
            last = kj == 3
            fn = lambda e, kj=kj, dk=dk: e.tensor_tensor(
                out=ebufs[kj], in0=dk[kj], in1=mbuf, op=OP.subtract)
            if last:
                P.inc("vector", s_ve, fn)
            else:
                P.emit("vector", fn)
        VSUB = s_ve.n
        P.wait("scalar", s_ve, VSUB)

        def sexp(e):
            for kj in range(4):
                i = e.activation(out=ebufs[kj], in_=ebufs[kj], func=AF.Exp,
                                 scale=1.0 / 32.0)
            return i
        P.inc("scalar", s_sc, sexp)
        SEXP = s_sc.n
        P.wait("vector", s_sc, SEXP)
        P.emit("vector", lambda e: e.tensor_tensor(
            out=mbuf, in0=ebufs[0], in1=ebufs[1], op=OP.add))
        P.emit("vector", lambda e: e.tensor_tensor(
            out=d2scr, in0=ebufs[2], in1=ebufs[3], op=OP.add))
        P.emit("vector", lambda e: e.tensor_tensor(
            out=mbuf, in0=mbuf, in1=d2scr, op=OP.add))
        P.emit("vector", lambda e: e.reciprocal(rmbuf, mbuf))
        P.emit("vector", lambda e, qi=qi: e.tensor_tensor(
            out=rmbuf, in0=rmbuf,
            in1=masks[:, 16 * qi:16 * (qi + 1)].unsqueeze(2
                ).broadcast_to([128, 16, 16]),
            op=OP.mult))
        for kj in range(4):
            P.emit("vector", lambda e, kj=kj: e.tensor_tensor(
                out=ebufs[kj], in0=ebufs[kj], in1=rmbuf, op=OP.mult))
            off = 16 * (kj - qi + 3)
            fn = lambda e, kj=kj, off=off: e.tensor_tensor(
                out=tapt[:, :, off:off + 16], in0=tapt[:, :, off:off + 16],
                in1=ebufs[kj], op=OP.add)
            if kj == 3:
                P.inc("vector", s_ve, fn)
            else:
                P.emit("vector", fn)
    TAPS = s_ve.n

    P.wait("gpsimd", s_ve, TAPS)
    for c in range(8):
        P.dma("gpsimd", s_wo, v_woutb[c], d_wout[128 * c:128 * (c + 1), :])
    WOUT = s_wo.n

    if SL < 5:
        _finish(nc, P, block_debug=(d_dbg, locals()))
        return nc
    D3t_ev = {}
    for tc in range(NTO):
        pbi = 4 + tc % 2
        if tc == 0:
            P.wait("tensor", s_ve, TAPS)
        if tc >= 2:
            P.wait("tensor", s_ve, D3t_ev[tc - 2])
        P.inc("tensor", s_pe, lambda e, tc=tc, pbi=pbi: e.transpose(
            PBh[pbi][0:112, 0:128], v_taptok[:, 112 * tc:112 * (tc + 1)],
            eyeb[:]))
        TP = s_pe.n
        P.wait("vector", s_pe, TP)
        P.inc("vector", s_ve, lambda e, tc=tc, pbi=pbi: e.tensor_copy(
            v_tapf[0:112, 128 * tc:128 * (tc + 1)], PBh[pbi][0:112, 0:128]))
        D3t_ev[tc] = s_ve.n
    TAPF = s_ve.n

    P.inc("vector", s_ve, lambda e: e.memset(AK[:, 0:8 * TOWN], 0.0))
    AOZ = s_ve.n

    P.wait("tensor", s_ve, TAPF)
    P.wait("tensor", s_eb, EBL)
    D3_ev = {}
    nb3 = 0
    for dlt in range(7):
        dv = dlt - 3
        for c in range(8):
            q = 16 * dlt + 2 * c
            for (toff, tn) in TT_OWN:
                pbi = 6 + nb3 % 2
                if nb3 >= 2:
                    P.wait("tensor", s_ve, D3_ev[nb3 - 2])
                nb3 += 1
                lhsT = AQ[0:112, 64 * q:64 * q + 128]
                P.inc("tensor", s_pe, lambda e, lhsT=lhsT, toff=toff, tn=tn,
                      pbi=pbi: e.matmul(PB[pbi][:, 0:tn], lhsT,
                                        v_tapf[0:112, toff:toff + tn],
                                        start=True, stop=True))
                BMM = s_pe.n
                P.wait("vector", s_pe, BMM)
                if nb3 == 1:
                    P.wait("vector", s_ve, AOZ)
                P.emit("vector", lambda e, c=c, dv=dv, toff=toff, tn=tn, pbi=pbi:
                       e.tensor_tensor(
                           out=tmp2[:, 0:tn], in0=PB[pbi][:, 0:tn],
                           in1=AV[:, LB * c + L0 + dv + toff:
                                  LB * c + L0 + dv + toff + tn],
                           op=OP.mult))
                P.inc("vector", s_ve, lambda e, c=c, toff=toff, tn=tn:
                      e.tensor_tensor(
                          out=AK[:, TOWN * c + toff: TOWN * c + toff + tn],
                          in0=AK[:, TOWN * c + toff: TOWN * c + toff + tn],
                          in1=tmp2[:, 0:tn], op=OP.add))
                D3_ev[nb3 - 1] = s_ve.n
    AO = s_ve.n

    fence()

    if SL < 6:
        _finish(nc, P, block_debug=(d_dbg, locals()))
        return nc
    P.wait("tensor", s_wo, WOUT)
    D4_ev = {}
    nb4 = 0
    for tc in range(NTO):
        for jc in range(2):
            pbi = nb4 % 2
            if nb4 >= 2:
                P.wait("tensor", s_ve, D4_ev[nb4 - 2])
            nb4 += 1

            def smm(e, tc=tc, jc=jc, pbi=pbi):
                for c in range(8):
                    e.matmul(PB[pbi][:, 0:512],
                             AK[:, TOWN * c + 128 * tc: TOWN * c + 128 * (tc + 1)],
                             v_woutb[c][:, 512 * jc:512 * (jc + 1)],
                             start=(c == 0), stop=False)
                return e.matmul(PB[pbi][:, 0:512],
                                countv[0:1, L0 + 128 * tc: L0 + 128 * (tc + 1)],
                                bouts[0:1, 512 * jc:512 * (jc + 1)],
                                start=False, stop=True)
            P.inc("tensor", s_pe, smm)
            SMM = s_pe.n
            P.wait("vector", s_pe, SMM)
            P.inc("vector", s_ve, lambda e, tc=tc, jc=jc, pbi=pbi: e.tensor_copy(
                AQ[:, 1024 * tc + 512 * jc: 1024 * tc + 512 * (jc + 1)],
                PB[pbi][:, 0:512]))
            D4_ev[nb4 - 1] = s_ve.n
    SDONE = s_ve.n

    fence()

    if SL < 7:
        _finish(nc, P, block_debug=(d_dbg, locals()))
        return nc
    E0gt_pe = {}
    nb5 = 0
    E0_b2ev = {}
    for tc in range(NTO):
        pbi = 2 + tc % 2
        if tc == 0:
            P.wait("tensor", s_ve, GATES)
        P.inc("tensor", s_pe, lambda e, tc=tc, pbi=pbi: e.transpose(
            PB[pbi][0:8, 0:128], gw[:, 8 * tc:8 * (tc + 1)], eyef[:]))
        GTP = s_pe.n
        P.wait("vector", s_pe, GTP)
        if tc >= 1:
            P.wait("vector", s_pe, E0gt_pe[tc - 1])
        P.inc("vector", s_ve, lambda e, pbi=pbi: e.tensor_copy(
            gT[:], PB[pbi][0:8, 0:128]))
        GTEV = s_ve.n
        P.wait("tensor", s_ve, GTEV)
        for jc in range(2):
            pbj = nb5 % 2
            if nb5 >= 2:
                P.wait("tensor", s_ve, E0_b2ev[nb5 - 2])
            nb5 += 1
            P.inc("tensor", s_pe, lambda e, jc=jc, pbj=pbj: e.matmul(
                PB[pbj][:, 0:512], gT[:], b2s[:, 512 * jc:512 * (jc + 1)],
                start=True, stop=True))
            B2MM = s_pe.n
            E0gt_pe[tc] = B2MM
            P.wait("vector", s_pe, B2MM)
            P.inc("vector", s_ve, lambda e, tc=tc, jc=jc, pbj=pbj: e.tensor_copy(
                AV[:, 1024 * tc + 512 * jc: 1024 * tc + 512 * (jc + 1)],
                PB[pbj][:, 0:512]))
            E0_b2ev[nb5 - 1] = s_ve.n

    fence()

    if SL < 8:
        _finish(nc, P, block_debug=(d_dbg, locals()))
        return nc
    GACC_HALF_V = [None]
    w1done = []
    w1_pre = {}

    def emit_w1_group(e_, fg):
        gidx = 16 * (2 * 0 + 0)
        gi = emit_w1_group.count
        emit_w1_group.count += 1
        if gi >= 2:
            P.wait("gpsimd", s_pe, w1done[gi - 2])
        for dc in range(8):
            P.dma("gpsimd", s_w1[fg % 2], wgrp(fg % 2, dc),
                  d_w1[e_, 128 * dc:128 * (dc + 1), 512 * fg:512 * (fg + 1)])
        return s_w1[fg % 2].n
    emit_w1_group.count = 0
    gelu_seq = []
    w2done = []
    P2_EV_PREV = 0
    PH2_PE_PREV = 0
    for e_ in range(NE):
        for tb in range(2):
            P.wait("scalar", s_pe, PH2_PE_PREV)
            first_p1 = True
            for fg in range(8):
                if (e_, tb, fg) in w1_pre:
                    W1L = w1_pre[(e_, tb, fg)]
                else:
                    W1L = emit_w1_group(e_, fg)

                P.wait("tensor", s_w1[fg % 2], W1L)
                if first_p1:
                    P.wait("tensor", s_ve, P2_EV_PREV)
                    first_p1 = False
                for fsub in range(4):
                    fc = 4 * fg + fsub
                    for th in range(2):
                        nbank = len(gelu_seq)
                        pbi = nbank % 4
                        if nbank >= 4:
                            P.wait("tensor", s_sc, gelu_seq[nbank - 4])

                        def hmm(e, e_=e_, tb=tb, fg=fg, fsub=fsub, th=th, pbi=pbi):
                            for dc in range(8):
                                i = e.matmul(
                                    PB[pbi][:, 0:512],
                                    wgrp(fg % 2, dc)[:, 128 * fsub:128 * (fsub + 1)],
                                    XF[:, LB * dc + L0 + 1024 * tb + 512 * th:
                                       LB * dc + L0 + 1024 * tb + 512 * (th + 1)],
                                    start=(dc == 0), stop=(dc == 7))
                            return i
                        P.inc("tensor", s_pe, hmm)
                        HMM = s_pe.n
                        P.wait("scalar", s_pe, HMM)
                        P.inc("scalar", s_sc, lambda e, e_=e_, fc=fc, th=th,
                              pbi=pbi: e.activation(
                                  out=h_view(fc, 512 * th, 512 * (th + 1)),
                                  in_=PB[pbi][:, 0:512], func=AF.Gelu,
                                  bias=b1s[:, 32 * e_ + fc:32 * e_ + fc + 1]))
                        gelu_seq.append(s_sc.n)
                w1done.append(s_pe.n)
            H_READY = s_sc.n

            for jc in range(2):
                for fc in range(32):
                    widx = len(w2done)
                    if widx >= 4:
                        P.wait("gpsimd", s_pe, w2done[widx - 4])
                    P.dma("gpsimd", s_w2[widx % 4], v_w2buf[widx % 4],
                          d_w2[e_, 128 * fc:128 * (fc + 1),
                               512 * jc:512 * (jc + 1)])
                    W2L = s_w2[widx % 4].n
                    if jc == 1 and fc in (16, 20) and (e_, tb) != (NE - 1, 1):
                        ne_, ntb = (e_, 1) if tb == 0 else (e_ + 1, 0)
                        nfg = 0 if fc == 16 else 1
                        w1_pre[(ne_, ntb, nfg)] = emit_w1_group(ne_, nfg)
                    P.wait("tensor", s_w2[widx % 4], W2L)
                    if fc == 0:
                        P.wait("tensor", s_sc, H_READY)
                        P.wait("tensor", s_ve, P2_EV_PREV)

                    def omm(e, jc=jc, fc=fc, widx=widx):
                        for tcc in range(8):
                            i = e.matmul(
                                PB[tcc][:, 0:512],
                                h_view(fc, 128 * tcc, 128 * (tcc + 1)),
                                v_w2buf[widx % 4][:, 0:512],
                                start=(fc == 0), stop=(fc == 31))
                        return i
                    P.inc("tensor", s_pe, omm)
                    w2done.append(s_pe.n)
                OMM = s_pe.n
                P.wait("vector", s_pe, OMM)
                for tcc in range(8):
                    tcg = 8 * tb + tcc
                    P.emit("vector", lambda e, e_=e_, tcg=tcg, tcc=tcc:
                           e.tensor_scalar(
                               out=tmp2[:, 0:512], in0=PB[tcc][:, 0:512],
                               scalar1=gw[:, 8 * tcg + e_:8 * tcg + e_ + 1],
                               scalar2=None, op0=OP.mult))
                    P.inc("vector", s_ve, lambda e, tcg=tcg, jc=jc:
                          e.tensor_tensor(
                              out=AV[:, 1024 * tcg + 512 * jc:
                                     1024 * tcg + 512 * (jc + 1)],
                              in0=AV[:, 1024 * tcg + 512 * jc:
                                     1024 * tcg + 512 * (jc + 1)],
                              in1=tmp2[:, 0:512], op=OP.add))
                P2_EV_PREV = s_ve.n
                if (e_, tb, jc) == (NE - 1, 0, 1):
                    GACC_HALF_V[0] = s_ve.n
            PH2_PE_PREV = s_pe.n
    GACC = s_ve.n
    GACC_HALF = GACC

    fence()

    if SL < 9:
        _finish(nc, P, block_debug=(d_dbg, locals()))
        return nc
    F_ev, F_mul = {}, {}
    for tc in range(NTO):
        bb = tc % 2
        ob = tc % 3
        if tc >= 2:
            P.wait("sync", s_ve, F_ev[tc - 2])
        P.dma("sync", s_x2[bb], v_xt2[bb][:],
              d_xh[L0 + 128 * tc: L0 + 128 * (tc + 1), :])
        X2 = s_x2[bb].n
        if tc == 0:
            P.wait("scalar", s_ve, GACC)
        if tc >= 2:
            P.wait("scalar", s_ve, F_mul[tc - 2])
        P.inc("scalar", s_sc, lambda e, tc=tc, bb=bb: e.activation(
            out=v_sg[bb][:], in_=AV[:, 1024 * tc:1024 * (tc + 1)], func=AF.Silu))
        SGD = s_sc.n
        P.wait("vector", s_sc, SGD)
        P.wait("vector", s_x2[bb], X2)
        if tc == 0:
            P.wait("vector", s_ve, SDONE)
        if tc >= 3:
            P.wait("vector", s_st[ob], s_st[ob].n)
        P.inc("vector", s_ve, lambda e, tc=tc, bb=bb, ob=ob: e.tensor_tensor(
            out=v_outb[ob][:], in0=v_sg[bb][:],
            in1=AQ[:, 1024 * tc:1024 * (tc + 1)], op=OP.mult))
        F_mul[tc] = s_ve.n
        P.inc("vector", s_ve, lambda e, bb=bb, ob=ob: e.tensor_tensor(
            out=v_outb[ob][:], in0=v_outb[ob][:], in1=v_xt2[bb][:], op=OP.add))
        F_ev[tc] = s_ve.n
        P.wait("sync", s_ve, F_ev[tc])
        P.dma("sync", s_st[ob], d_out[128 * tc:128 * (tc + 1), :], v_outb[ob][:])

    if d_dbg:
        ctx = dict(XF=XF, AQ=AQ, AK=AK, AV=AV, H2=H2, WS=WS, glog=glog, gw=gw,
                   v_ddf=v_ddf, v_ddtok=v_ddtok, v_taptok=v_taptok,
                   v_tapf=v_tapf, STAT=STAT, sbc=sbc, H2f=H2f)
        P.wait("sync", s_ve, s_ve.n)
        P.wait("sync", s_sc, s_sc.n)
        for name, ap in d_dbg.items():
            P.dma("sync", s_st[0], ap, DBG_SRC[name](ctx))

    for _s in s_st:
        P.wait("sync", _s, _s.n)

    with nc.allow_low_precision("bf16 accumulators: 4-term softmax/tap sums, "
                                "ao 7-tap sum, g_acc 2-expert sum (tolerance "
                                "validated)"):
        with nc.Block() as block:
            P.run(block)
    return nc


def _host_prep(L0_attn_x, L0_attn_w_qkv, L0_attn_w_out, L0_attn_b_out,
               L0_ffn_centroids, L0_ffn_W1, L0_ffn_b1, L0_ffn_W2, L0_ffn_b2):
    bf = ml_dtypes.bfloat16
    x = np.ascontiguousarray(np.asarray(L0_attn_x, dtype=np.float32))
    wqkv = np.ascontiguousarray(np.asarray(L0_attn_w_qkv, dtype=np.float32))
    wout = np.ascontiguousarray(np.asarray(L0_attn_w_out, dtype=np.float32))
    bout = np.ascontiguousarray(np.asarray(L0_attn_b_out, dtype=np.float32)
                                .reshape(1, D))
    cent = np.asarray(L0_ffn_centroids, dtype=np.float32)
    ctt = np.ascontiguousarray(cent.T)
    w1 = np.ascontiguousarray(np.asarray(L0_ffn_W1, dtype=np.float32))
    b1 = np.asarray(L0_ffn_b1, dtype=np.float32)
    b1t = np.ascontiguousarray(
        b1.reshape(NE, 32, 128).transpose(2, 0, 1).reshape(128, NE * 32))
    w2 = np.ascontiguousarray(np.asarray(L0_ffn_W2, dtype=np.float32))
    b2 = np.ascontiguousarray(np.asarray(L0_ffn_b2, dtype=np.float32))

    eyeb = np.eye(128, dtype=bf)
    eyef = np.eye(128, dtype=np.float32)
    bonesA = np.zeros((128, 256), dtype=np.float32)
    bonesB = np.zeros((128, 256), dtype=np.float32)
    for c in range(8):
        for hh in range(16):
            if hh // 2 == c:
                rows = slice(64 * (hh % 2), 64 * (hh % 2 + 1))
                bonesA[rows, 32 * c + hh] = 1.0
                bonesB[rows, 32 * c + 16 + hh] = 1.0
    bonesA = bonesA.astype(bf)
    bonesB = bonesB.astype(bf)
    eb = np.zeros((128, 7168), dtype=np.float32)
    for kk in range(112):
        eb[kk, 64 * kk:64 * kk + 64] = 1.0
    eb = eb.astype(bf)
    eone = np.zeros((128, 224), dtype=np.float32)
    for kk in range(112):
        eone[kk, 2 * kk] = 1.0
        if 2 * kk + 1 < 224:
            eone[kk, 2 * kk + 1] = 1.0
    eone = eone.astype(bf)

    wmax = (SEQ - 4) // 2
    in_maps = []
    for core in range(NC):
        bb, j = divmod(core, 4)
        t0 = TOWN * j
        lo, hi = t0 - L0, t0 + TOWN + L0
        xs = x[bb, max(0, lo):min(SEQ, hi)]
        xh = np.zeros((LB, D), dtype=np.float32)
        xh[max(0, -lo): LB - max(0, hi - SEQ)] = xs
        tg = t0 + np.arange(-L0, TOWN + L0)
        wlo = np.maximum(np.ceil((tg - 3) / 2.0), 0)
        whi = np.minimum(np.floor(tg / 2.0), wmax)
        cv = np.maximum(whi - wlo + 1, 0).astype(np.float32)
        countv = cv.reshape(1, LB)[:, :2176].astype(bf)
        masks = np.zeros((128, 64), dtype=np.float32)
        for qi in range(4):
            for tc in range(16):
                t = t0 + 128 * tc + np.arange(128)
                ok = (t % 2) == (qi % 2)
                w = (t - qi) // 2
                ok &= (w >= 0) & (w <= wmax)
                masks[:, 16 * qi + tc] = ok.astype(np.float32)
        desc8 = np.tile((8.0 - np.arange(8, dtype=np.float32)), 16)
        desc8 = np.tile(desc8.reshape(1, 128), (128, 1))
        in_maps.append({
            "xh": xh, "wqkv": wqkv, "wout": wout, "bout": bout, "ctt": ctt,
            "desc8": desc8,
            "w1": w1, "b1t": b1t, "w2": w2, "b2": b2,
            "eyeb": eyeb, "eyef": eyef, "bonesA": bonesA, "bonesB": bonesB,
            "eone": eone, "eb": eb,
            "masks": masks, "countv": countv,
        })
    return in_maps


def kernel(L0_attn_x, L0_attn_w_qkv, L0_attn_w_out, L0_attn_b_out,
           L0_ffn_centroids, L0_ffn_W1, L0_ffn_b1, L0_ffn_W2, L0_ffn_b2):
    in_maps = _host_prep(L0_attn_x, L0_attn_w_qkv, L0_attn_w_out, L0_attn_b_out,
                         L0_ffn_centroids, L0_ffn_W1, L0_ffn_b1, L0_ffn_W2,
                         L0_ffn_b2)
    nc = build_program(DBG.items())
    res = run_bass_kernel_spmd(nc, in_maps, core_ids=list(range(NC)))
    out = np.empty((B, SEQ, D), dtype=np.float32)
    for core in range(NC):
        bb, j = divmod(core, 4)
        out[bb, TOWN * j:TOWN * (j + 1)] = res.results[core]["out"]
    if DBG:
        kernel.dbg = [{k: r["dbg_" + k] for k in dict(DBG)}
                      for r in res.results]
    return out


# version 11
# speedup vs baseline: 1.1384x; 1.1384x over previous
import numpy as np
import ml_dtypes
import concourse.bass as bass
import concourse.mybir as mybir
from concourse.bass_utils import run_bass_kernel_spmd
from contextlib import ExitStack

F32 = mybir.dt.float32
BF16 = mybir.dt.bfloat16
AF = mybir.ActivationFunctionType
OP = mybir.AluOpType
AX = mybir.AxisListType

D = 1024
DFF = 4096
NE = 8
SEQ = 8192
B = 2
TOWN = 2048
L0 = 128
LB = 2304
NT = LB // 128
NTO = TOWN // 128
NC = 8
EPS = 1e-5

TT_FULL = [(0, 512), (512, 512), (1024, 512), (1536, 512), (2048, 256)]
TT_OWN = [(0, 512), (512, 512), (1024, 512), (1536, 512)]

ENGS = ("sync", "gpsimd", "scalar", "vector", "tensor")

DBG = {}
DBG_SRC = {}
STAGE_LIMIT = 99


class Sem:
    def __init__(self, nc, es, name):
        self.h = es.enter_context(nc.semaphore(name))
        self.n = 0


class Prog:

    def __init__(self):
        self.streams = {e: [] for e in ENGS}

    def emit(self, eng, fn):
        self.streams[eng].append(fn)

    def wait(self, eng, sem, val):
        h = sem.h
        self.emit(eng, lambda e: e.wait_ge(h, val))

    def inc(self, eng, sem, buildf, k=1):
        sem.n += k
        h = sem.h
        self.emit(eng, lambda e: buildf(e).then_inc(h, k))

    def dma(self, eng, sem, out, in_):
        sem.n += 16
        h = sem.h
        self.emit(eng, lambda e: e.dma_start(out=out, in_=in_).then_inc(h, 16))

    def run(self, block):
        for eng in ENGS:
            thunks = self.streams[eng]
            def body(e, thunks=thunks):
                for f in thunks:
                    f(e)
            getattr(block, eng)(body)



def _finish(nc, P, block_debug):
    d_dbg, loc = block_debug
    if d_dbg:
        ctx = {k: loc[k] for k in ("XF", "AQ", "AK", "AV", "H2", "glog", "gw",
                                   "v_ddf", "v_ddtok", "v_taptok", "v_tapf",
                                   "STAT", "sbc", "H2f", "WS")
               if k in loc}
        s_ve = loc["s_ve"]; s_sc = loc["s_sc"]; s_pe = loc["s_pe"]
        s_st = loc["s_st"][0]
        P.wait("sync", s_ve, s_ve.n)
        P.wait("sync", s_sc, s_sc.n)
        P.wait("sync", s_pe, s_pe.n)
        for name, ap in d_dbg.items():
            P.dma("sync", s_st, ap, DBG_SRC[name](ctx))
        P.wait("sync", s_st, s_st.n)
    with nc.allow_low_precision("bf16 accumulators (see build_program)"):
        with nc.Block() as block:
            P.run(block)


def build_program(debug_taps=()):
    nc = bass.Bass("TRN2", target_bir_lowering=False)
    es = ExitStack()

    d_xh = nc.dram_tensor("xh", [LB, D], F32, kind="ExternalInput").ap()
    d_wqkv = nc.dram_tensor("wqkv", [D, 3 * D], F32, kind="ExternalInput").ap()
    d_wout = nc.dram_tensor("wout", [D, D], F32, kind="ExternalInput").ap()
    d_bout = nc.dram_tensor("bout", [1, D], F32, kind="ExternalInput").ap()
    d_ctt = nc.dram_tensor("ctt", [D, NE], F32, kind="ExternalInput").ap()
    d_w1 = nc.dram_tensor("w1", [NE, D, DFF], F32, kind="ExternalInput").ap()
    d_b1t = nc.dram_tensor("b1t", [128, NE * 32], F32, kind="ExternalInput").ap()
    d_w2 = nc.dram_tensor("w2", [NE, DFF, D], F32, kind="ExternalInput").ap()
    d_b2 = nc.dram_tensor("b2", [NE, D], F32, kind="ExternalInput").ap()
    d_eyeb = nc.dram_tensor("eyeb", [128, 128], BF16, kind="ExternalInput").ap()
    d_eyef = nc.dram_tensor("eyef", [128, 128], F32, kind="ExternalInput").ap()
    d_bonesA = nc.dram_tensor("bonesA", [128, 256], BF16, kind="ExternalInput").ap()
    d_bonesB = nc.dram_tensor("bonesB", [128, 256], BF16, kind="ExternalInput").ap()
    d_eone = nc.dram_tensor("eone", [128, 224], BF16, kind="ExternalInput").ap()
    d_eb = nc.dram_tensor("eb", [128, 7168], BF16, kind="ExternalInput").ap()
    d_masks = nc.dram_tensor("masks", [128, 64], F32, kind="ExternalInput").ap()
    d_desc8 = nc.dram_tensor("desc8", [128, 128], F32, kind="ExternalInput").ap()
    d_countv = nc.dram_tensor("countv", [1, 2176], BF16, kind="ExternalInput").ap()
    d_out = nc.dram_tensor("out", [TOWN, D], F32, kind="ExternalOutput").ap()
    d_dbg = {}
    for name, (shape, dt) in dict(debug_taps).items():
        d_dbg[name] = nc.dram_tensor("dbg_" + name, list(shape), dt,
                                     kind="ExternalOutput").ap()

    t_XF = es.enter_context(nc.sbuf_tensor("XF", [128, 8 * LB], BF16))
    t_AQ = es.enter_context(nc.sbuf_tensor("AQ", [128, 8 * LB], BF16))
    t_AK = es.enter_context(nc.sbuf_tensor("AK", [128, 8 * LB], BF16))
    t_AV = es.enter_context(nc.sbuf_tensor("AV", [128, 8 * LB], BF16))
    t_H2 = es.enter_context(nc.sbuf_tensor("H2", [128, 14336], BF16))
    t_WS = es.enter_context(nc.sbuf_tensor("WS", [128, 10240], BF16))
    XF, AQ, AK, AV, H2, WS = (t.ap() for t in (t_XF, t_AQ, t_AK, t_AV, t_H2, t_WS))
    AVf = AV.bitcast(F32)
    H2f = H2.bitcast(F32)

    small = {}
    for nm, shape, dt in [
            ("eyeb", [128, 128], BF16), ("eyef", [128, 128], F32),
            ("bonesA", [128, 256], BF16), ("bonesB", [128, 256], BF16),
            ("eone", [128, 224], BF16),
            ("masks", [128, 64], F32), ("countv", [1, 2176], BF16),
            ("desc8", [128, 128], F32),
            ("ctf", [128, 64], F32), ("b1s", [128, NE * 32], F32),
            ("b2s", [NE, D], BF16), ("bouts", [1, D], BF16),
            ("glog", [128, 128], F32), ("gw", [128, 128], F32),
            ("sbc", [128, 8], F32), ("srow", [1, 8], F32),
            ("ones1", [1, 128], F32), ("ones128", [128, 1], F32),
            ("gT", [NE, 128], BF16), ("tmp2", [128, 512], BF16),
            ("epsv", [128, 1], F32),
            ("stat", [128, 16], F32)]:
        small[nm] = es.enter_context(nc.sbuf_tensor("sb_" + nm, shape, dt)).ap()
    eyeb, eyef = small["eyeb"], small["eyef"]
    bonesA, bonesB, eone = small["bonesA"], small["bonesB"], small["eone"]
    masks, countv, ctf, b1s = (small[x] for x in ("masks", "countv", "ctf", "b1s"))
    b2s, bouts, glog, gw = (small[x] for x in ("b2s", "bouts", "glog", "gw"))
    sbc, srow, ones1, ones128 = (small[x] for x in
                                 ("sbc", "srow", "ones1", "ones128"))
    gT, tmp2, STAT = small["gT"], small["tmp2"], small["stat"]

    v_xt = [AVf[:, 0:1024], AVf[:, 1024:2048]]
    v_sq = AVf[:, 2048:3072]
    v_xtr = [[AVf[:, 3072 + bb * 1024 + c * 128: 3072 + bb * 1024 + (c + 1) * 128]
              for c in range(8)] for bb in range(2)]
    v_xnt = [AV[:, 10240:11264], AV[:, 11264:12288]]

    v_P = [H2[:, 0:2048], H2[:, 2048:4096]]
    v_ddf = H2[:, 4608:4608 + 2048]
    v_ddtok = H2[:, 6656:6656 + 1792]
    v_taptok = H2[:, 8448:8448 + 1792]
    v_tapf = H2[:, 10240:10240 + 2048]
    v_sm = [H2[:, 12288 + i * 256: 12288 + (i + 1) * 256] for i in range(8)]
    v_woutb = [H2[:, 1024 * c: 1024 * (c + 1)] for c in range(8)]
    v_lgscr = v_sm[0].bitcast(F32)

    def h_view(fc, c0, c1):
        if fc < 18:
            return AK[:, 1024 * fc + c0: 1024 * fc + c1]
        return H2[:, 1024 * (fc - 18) + c0: 1024 * (fc - 18) + c1]

    v_xt2 = [H2f[:, 0:1024], H2f[:, 1024:2048]]
    v_sg = [H2f[:, 2048:3072], H2f[:, 3072:4096]]
    v_outb = [H2f[:, 4096:5120], H2f[:, 5120:6144], H2f[:, 6144:7168]]
    v_tmpf = H2f[:, 3072:4096]


    def wgrp(bb, dc):
        return WS[:, 4096 * bb + 512 * dc: 4096 * bb + 512 * (dc + 1)]
    v_w2buf = [WS[:, 8192 + 512 * i: 8192 + 512 * (i + 1)] for i in range(4)]

    PB = [es.enter_context(nc.psum_tensor("pb%d" % i, [128, 512], F32)).ap()
          for i in range(8)]
    PBh = [p.bitcast(BF16) for p in PB]

    SL = STAGE_LIMIT
    P = Prog()
    s_x = [Sem(nc, es, "s_x0"), Sem(nc, es, "s_x1")]
    s_c = Sem(nc, es, "s_c")
    s_wq = [Sem(nc, es, "s_wq0"), Sem(nc, es, "s_wq1")]
    s_wo = Sem(nc, es, "s_wo")
    s_eb = Sem(nc, es, "s_eb")
    s_w1 = [Sem(nc, es, "s_w1a"), Sem(nc, es, "s_w1b")]
    s_w2 = [Sem(nc, es, "s_w2%d" % i) for i in range(4)]
    s_sc = Sem(nc, es, "s_sc")
    s_ve = Sem(nc, es, "s_ve")
    s_pe = Sem(nc, es, "s_pe")
    s_st = [Sem(nc, es, "s_st%d" % i) for i in range(3)]
    s_x2 = [Sem(nc, es, "s_x2a"), Sem(nc, es, "s_x2b")]

    def fence(engs=("tensor", "vector", "scalar", "sync")):
        vals = [(s_pe, s_pe.n), (s_ve, s_ve.n), (s_sc, s_sc.n)]
        for eng in engs:
            for sem, v in vals:
                P.wait(eng, sem, v)

    s_vb = Sem(nc, es, "s_vb")
    vbs = small["stat"][:, 14:16]

    def vbar():
        P.inc("vector", s_vb, lambda e: e.tensor_copy(vbs[0:1, 0:1],
                                                      vbs[0:1, 1:2]))
        P.wait("vector", s_vb, s_vb.n)

    P.dma("gpsimd", s_c, small["eyeb"][:], d_eyeb[:])
    P.dma("gpsimd", s_c, small["eyef"][:], d_eyef[:])
    P.dma("gpsimd", s_c, bonesA[:], d_bonesA[:])
    P.dma("gpsimd", s_c, bonesB[:], d_bonesB[:])
    P.dma("gpsimd", s_c, eone[:], d_eone[:])
    P.dma("gpsimd", s_c, masks[:], d_masks[:])
    P.dma("gpsimd", s_c, small["desc8"][:], d_desc8[:])
    P.dma("gpsimd", s_c, countv[:], d_countv[:])
    P.dma("gpsimd", s_c, b1s[:], d_b1t[:])
    P.dma("gpsimd", s_c, b2s[:], d_b2[:])
    P.dma("gpsimd", s_c, bouts[:], d_bout[:])
    for dc in range(8):
        P.dma("gpsimd", s_c, ctf[:, 8 * dc:8 * dc + 8],
              d_ctt[128 * dc:128 * (dc + 1), :])
    NCONST = s_c.n

    P.emit("vector", lambda e: e.memset(ones1[:], 1.0))
    P.emit("vector", lambda e: e.memset(small["epsv"][:], EPS))
    P.inc("vector", s_ve, lambda e: e.memset(ones128[:], 1.0))
    VE_ONES = s_ve.n

    P.wait("tensor", s_c, NCONST)
    P.wait("tensor", s_ve, VE_ONES)

    def mm_srow(e):
        for dc in range(8):
            i = e.matmul(PB[6][0:1, 0:8], ones128[:], ctf[:, 8 * dc:8 * dc + 8],
                         start=(dc == 0), stop=(dc == 7))
        return i
    P.inc("tensor", s_pe, mm_srow)
    P.wait("vector", s_pe, s_pe.n)
    P.inc("vector", s_ve, lambda e: e.tensor_copy(srow[:], PB[6][0:1, 0:8]))
    P.wait("tensor", s_ve, s_ve.n)
    P.inc("tensor", s_pe, lambda e: e.matmul(PB[7][:, 0:8], ones1[:], srow[:],
                                             start=True, stop=True))
    P.wait("vector", s_pe, s_pe.n)
    P.inc("vector", s_ve, lambda e: e.tensor_copy(sbc[:], PB[7][:, 0:8]))
    VE_SBC = s_ve.n

    A_xn, A_bt, A_gmm, A_glog = {}, {}, {}, {}
    A_bt_ev, A_ft_ev = {}, {}
    for k in range(NT):
        bb = k % 2
        owned = 1 <= k <= NTO
        if k >= 2:
            P.wait("sync", s_ve, A_xn[k - 2])
            P.wait("sync", s_pe, A_gmm.get(k - 2, A_bt.get(k - 2, 0)))
        P.dma("sync", s_x[bb], v_xt[bb][:], d_xh[128 * k:128 * (k + 1), :])
        XLD = s_x[bb].n

        P.wait("scalar", s_x[bb], XLD)
        P.emit("scalar", lambda e, bb=bb: e.activation(
            out=v_sq[:], in_=v_xt[bb][:], func=AF.Square,
            accum_out=STAT[:, 2 + bb:3 + bb]))
        P.inc("scalar", s_sc, lambda e, bb=bb: e.activation(
            out=v_sq[:], in_=v_xt[bb][:], func=AF.Copy,
            accum_out=STAT[:, 0 + bb:1 + bb]))
        ASTAT = s_sc.n

        P.wait("vector", s_sc, ASTAT)
        P.emit("vector", lambda e, bb=bb: e.tensor_scalar(
            out=STAT[:, 6 + bb:7 + bb], in0=STAT[:, 2 + bb:3 + bb],
            scalar1=1.0 / D, scalar2=None, op0=OP.mult))
        P.emit("vector", lambda e, bb=bb: e.tensor_scalar(
            out=STAT[:, 4 + bb:5 + bb], in0=STAT[:, 0 + bb:1 + bb],
            scalar1=1.0 / D, scalar2=None, op0=OP.mult))
        vbar()
        P.inc("vector", s_ve, lambda e, bb=bb: e.scalar_tensor_tensor(
            out=STAT[:, 6 + bb:7 + bb], in0=STAT[:, 4 + bb:5 + bb],
            scalar=STAT[:, 4 + bb:5 + bb], in1=STAT[:, 6 + bb:7 + bb],
            op0=OP.mult, op1=OP.subtract))
        AVAR = s_ve.n
        P.wait("scalar", s_ve, AVAR)
        P.inc("scalar", s_sc, lambda e, bb=bb: e.activation(
            out=STAT[:, 8 + bb:9 + bb], in_=STAT[:, 6 + bb:7 + bb],
            func=AF.Sqrt, bias=small["epsv"][:], scale=-1.0))
        ASD = s_sc.n
        P.wait("vector", s_sc, ASD)
        P.emit("vector", lambda e, bb=bb: e.reciprocal(
            STAT[:, 10 + bb:11 + bb], STAT[:, 8 + bb:9 + bb]))
        vbar()
        P.inc("vector", s_ve, lambda e, bb=bb: e.tensor_scalar(
            out=v_xnt[bb][:], in0=v_xt[bb][:],
            scalar1=STAT[:, 4 + bb:5 + bb], scalar2=STAT[:, 10 + bb:11 + bb],
            op0=OP.subtract, op1=OP.mult))
        A_xn[k] = s_ve.n

        P.wait("tensor", s_ve, A_xn[k])
        for c in range(8):
            pbi = 4 + (c % 2)
            prev = A_bt_ev.get((k, c - 2), A_bt_ev.get((k - 1, 6 + (c % 2))))
            if prev:
                P.wait("tensor", s_ve, prev)
            P.inc("tensor", s_pe, lambda e, bb=bb, c=c, pbi=pbi: e.transpose(
                PBh[pbi][:, 0:128], v_xnt[bb][:, 128 * c:128 * (c + 1)], eyeb[:]))
            A_bt[k] = s_pe.n
            P.wait("vector", s_pe, A_bt[k])
            P.inc("vector", s_ve, lambda e, k=k, c=c, pbi=pbi: e.tensor_copy(
                XF[:, LB * c + 128 * k: LB * c + 128 * (k + 1)],
                PBh[pbi][:, 0:128]))
            A_bt_ev[(k, c)] = s_ve.n
        if owned:
            for c in range(8):
                pbi = 2 + (c % 2)
                prev = A_ft_ev.get((k, c - 2), A_ft_ev.get((k - 1, 6 + (c % 2))))
                if prev:
                    P.wait("tensor", s_ve, prev)
                P.inc("tensor", s_pe, lambda e, bb=bb, c=c, pbi=pbi: e.transpose(
                    PB[pbi][:, 0:128], v_xt[bb][:, 128 * c:128 * (c + 1)], eyef[:]))
                AFT = s_pe.n
                P.wait("vector", s_pe, AFT)
                P.inc("vector", s_ve, lambda e, bb=bb, c=c, pbi=pbi: e.tensor_copy(
                    v_xtr[bb][c][:], PB[pbi][:, 0:128]))
                A_ft_ev[(k, c)] = s_ve.n
            gb = (k - 1) % 2
            P.wait("tensor", s_ve, A_ft_ev[(k, 7)])
            if k >= 3:
                P.wait("tensor", s_ve, A_glog[k - 2])

            def gmm(e, bb=bb, gb=gb):
                for dc in range(8):
                    i = e.matmul(PB[gb][:, 0:8], v_xtr[bb][dc][:],
                                 ctf[:, 8 * dc:8 * dc + 8],
                                 start=(dc == 0), stop=(dc == 7))
                return i
            P.inc("tensor", s_pe, gmm)
            A_gmm[k] = s_pe.n
            P.wait("vector", s_pe, A_gmm[k])
            if k == 1:
                P.wait("vector", s_ve, VE_SBC)
            P.emit("vector", lambda e, bb=bb, gb=gb: e.scalar_tensor_tensor(
                out=v_lgscr[:, 0:8], in0=sbc[:],
                scalar=STAT[:, 4 + bb:5 + bb], in1=PB[gb][:, 0:8],
                op0=OP.mult, op1=OP.subtract))
            vbar()
            P.inc("vector", s_ve, lambda e, k=k, bb=bb: e.tensor_scalar(
                out=glog[:, 8 * (k - 1):8 * k], in0=v_lgscr[:, 0:8],
                scalar1=STAT[:, 10 + bb:11 + bb], scalar2=-1.0,
                op0=OP.mult, op1=OP.mult))
            A_glog[k] = s_ve.n

    P.wait("scalar", s_ve, A_glog[NTO])
    P.inc("scalar", s_sc, lambda e: e.activation(out=v_lgscr[:, 0:128],
                                                 in_=glog[:], func=AF.Exp,
                                                 scale=-1.0))
    SIG = s_sc.n
    P.wait("vector", s_sc, SIG)
    P.emit("vector", lambda e: e.tensor_scalar(
        out=v_lgscr[:, 0:128], in0=v_lgscr[:, 0:128], scalar1=1.0,
        scalar2=None, op0=OP.add))
    P.emit("vector", lambda e: e.reciprocal(glog[:], v_lgscr[:, 0:128]))
    gv = glog.rearrange("p (t e) -> p t e", e=8)
    gwv = gw.rearrange("p (t e) -> p t e", e=8)
    mmf = v_sm[1].bitcast(F32)
    m1 = mmf[:, 0:16]
    t1 = mmf[:, 16:32]
    m2v = mmf[:, 32:48]
    t2 = mmf[:, 48:64]
    den = mmf[:, 64:80]
    rden = mmf[:, 80:96]
    scr = v_sm[2].bitcast(F32)[:, 0:128].rearrange("p (t e) -> p t e", e=8)
    scr2 = v_sm[3].bitcast(F32)[:, 0:128].rearrange("p (t e) -> p t e", e=8)
    descv = small["desc8"].rearrange("p (t e) -> p t e", e=8)

    def bc16(ap):
        return ap.unsqueeze(2).broadcast_to([128, 16, 8])

    P.emit("vector", lambda e: e.tensor_reduce(out=m1[:], in_=gv, axis=AX.X,
                                               op=OP.max))
    vbar()
    P.emit("vector", lambda e: e.tensor_tensor(out=scr, in0=gv, in1=bc16(m1),
                                               op=OP.is_equal))
    P.emit("vector", lambda e: e.tensor_tensor(out=scr, in0=scr, in1=descv,
                                               op=OP.mult))
    vbar()
    P.emit("vector", lambda e: e.tensor_reduce(out=t1[:], in_=scr, axis=AX.X,
                                               op=OP.max))
    vbar()
    P.emit("vector", lambda e: e.tensor_tensor(out=scr, in0=scr, in1=bc16(t1),
                                               op=OP.is_equal))
    vbar()
    P.emit("vector", lambda e: e.scalar_tensor_tensor(
        out=scr2, in0=scr, scalar=-1e30, in1=gv, op0=OP.mult, op1=OP.add))
    vbar()
    P.emit("vector", lambda e: e.tensor_reduce(out=m2v[:], in_=scr2, axis=AX.X,
                                               op=OP.max))
    vbar()
    P.emit("vector", lambda e: e.tensor_tensor(out=scr2, in0=scr2, in1=bc16(m2v),
                                               op=OP.is_equal))
    P.emit("vector", lambda e: e.tensor_tensor(out=scr2, in0=scr2, in1=descv,
                                               op=OP.mult))
    vbar()
    P.emit("vector", lambda e: e.tensor_reduce(out=t2[:], in_=scr2, axis=AX.X,
                                               op=OP.max))
    vbar()
    P.emit("vector", lambda e: e.tensor_tensor(out=scr2, in0=scr2, in1=bc16(t2),
                                               op=OP.is_equal))
    vbar()
    P.emit("vector", lambda e: e.tensor_tensor(out=scr, in0=scr, in1=scr2,
                                               op=OP.add))
    P.emit("vector", lambda e: e.tensor_tensor(out=den[:], in0=m1[:], in1=m2v[:],
                                               op=OP.add))
    vbar()
    P.emit("vector", lambda e: e.reciprocal(rden[:], den[:]))
    vbar()
    P.emit("vector", lambda e: e.tensor_tensor(out=gwv, in0=gv, in1=scr,
                                               op=OP.mult))
    P.inc("vector", s_ve, lambda e: e.tensor_tensor(
        out=gwv, in0=gwv, in1=bc16(rden), op=OP.mult))
    GATES = s_ve.n

    fence()

    if SL < 2:
        _finish(nc, P, block_debug=(d_dbg, locals()))
        return nc
    C_grp_pe = {}
    C_ev = []
    ngC = 0
    for jg in range(6):
        if jg >= 2:
            P.wait("gpsimd", s_pe, C_grp_pe[jg - 2])
        for dc in range(8):
            P.dma("gpsimd", s_wq[jg % 2], wgrp(jg % 2, dc),
                  d_wqkv[128 * dc:128 * (dc + 1), 512 * jg:512 * (jg + 1)])
        WQL = s_wq[jg % 2].n

        P.wait("tensor", s_wq[jg % 2], WQL)
        for jci in range(4):
            j = 512 * jg + 128 * jci
            for (toff, tn) in TT_FULL:
                pbi = ngC % 4
                if ngC >= 4:
                    P.wait("tensor", s_ve, C_ev[ngC - 4])
                ngC += 1

                def qmm(e, jg=jg, jci=jci, toff=toff, tn=tn, pbi=pbi):
                    for dc in range(8):
                        i = e.matmul(
                            PB[pbi][:, 0:tn],
                            wgrp(jg % 2, dc)[:, 128 * jci:128 * (jci + 1)],
                            XF[:, LB * dc + toff: LB * dc + toff + tn],
                            start=(dc == 0), stop=(dc == 7))
                    return i
                P.inc("tensor", s_pe, qmm)
                MM = s_pe.n
                C_grp_pe[jg] = MM
                c = (j % 1024) // 128
                dst = (AQ, AK, AV)[j // 1024]
                P.wait("vector", s_pe, MM)
                P.inc("vector", s_ve, lambda e, dst=dst, c=c, toff=toff, tn=tn,
                      pbi=pbi: e.tensor_copy(
                          dst[:, LB * c + toff: LB * c + toff + tn],
                          PB[pbi][:, 0:tn]))
                C_ev.append(s_ve.n)

    fence()

    if SL < 3:
        _finish(nc, P, block_debug=(d_dbg, locals()))
        return nc
    D1_pe, D1_ev = {}, {}
    for dp in range(4):
        dls = [2 * dp] + ([2 * dp + 1] if dp < 3 else [])
        for c in range(8):
            prev = D1_pe.get((dp, c - 1), D1_pe.get((dp - 1, 7)))
            if prev:
                P.wait("vector", s_pe, prev)
            for li, dlt in enumerate(dls):
                dv = dlt - 3
                P.inc("vector", s_ve, lambda e, c=c, dv=dv, li=li: e.tensor_tensor(
                    out=v_P[li][:, 0:TOWN],
                    in0=AQ[:, LB * c + L0: LB * c + L0 + TOWN],
                    in1=AK[:, LB * c + L0 + dv: LB * c + L0 + dv + TOWN],
                    op=OP.mult))
            PDONE = s_ve.n
            P.wait("tensor", s_ve, PDONE)
            if c == 0 and dp > 0:
                P.wait("tensor", s_ve, D1_ev[dp - 1])

            def ddmm(e, c=c, dls=dls):
                for tt in range(4):
                    for li, dlt in enumerate(dls):
                        bm = (bonesA, bonesB)[li]
                        i = e.matmul(PB[tt][0:32, 0:512],
                                     bm[:, 32 * c:32 * (c + 1)],
                                     v_P[li][:, 512 * tt:512 * (tt + 1)],
                                     start=(c == 0 and li == 0),
                                     stop=(c == 7 and li == len(dls) - 1))
                return i
            P.inc("tensor", s_pe, ddmm)
            D1_pe[(dp, c)] = s_pe.n
        P.wait("vector", s_pe, D1_pe[(dp, 7)])
        nrow = 16 * len(dls)
        for tt in range(4):
            P.inc("vector", s_ve, lambda e, dp=dp, tt=tt, nrow=nrow: e.tensor_copy(
                v_ddf[32 * dp:32 * dp + nrow, 512 * tt:512 * (tt + 1)],
                PB[tt][0:nrow, 0:512]))
        D1_ev[dp] = s_ve.n
    K_DONE = s_pe.n

    if SL < 4:
        _finish(nc, P, block_debug=(d_dbg, locals()))
        return nc
    P.wait("gpsimd", s_ve, s_ve.n)
    P.dma("gpsimd", s_eb, AQ[:, 0:7168], d_eb[:])
    EBL = s_eb.n

    D15_ev = {}
    for tc in range(NTO):
        pbi = 4 + tc % 2
        if tc >= 2:
            P.wait("tensor", s_ve, D15_ev[tc - 2])
        if tc < 2:
            P.wait("tensor", s_ve, D1_ev[3])
        P.inc("tensor", s_pe, lambda e, tc=tc, pbi=pbi: e.transpose(
            PBh[pbi][:, 0:128], v_ddf[0:112, 128 * tc:128 * (tc + 1)],
            eyeb[0:112, :]))
        TP = s_pe.n
        P.wait("vector", s_pe, TP)
        P.inc("vector", s_ve, lambda e, tc=tc, pbi=pbi: e.tensor_copy(
            v_ddtok[:, 112 * tc:112 * (tc + 1)], PBh[pbi][:, 0:112]))
        D15_ev[tc] = s_ve.n

    P.inc("vector", s_ve, lambda e: e.memset(v_taptok[:], 0.0))
    ddt = v_ddtok.rearrange("p (t x) -> p t x", x=112)
    tapt = v_taptok.rearrange("p (t x) -> p t x", x=112)
    ebufs = [v_sm[i].rearrange("p (t h) -> p t h", h=16) for i in range(4)]
    mbuf = v_sm[4].rearrange("p (t h) -> p t h", h=16)
    rmbuf = v_sm[5].rearrange("p (t h) -> p t h", h=16)
    d2scr = v_sm[6].rearrange("p (t h) -> p t h", h=16)

    for qi in range(4):
        dk = [ddt[:, :, 16 * (kj - qi + 3): 16 * (kj - qi + 3) + 16]
              for kj in range(4)]
        P.emit("vector", lambda e, dk=dk: e.tensor_tensor(
            out=mbuf, in0=dk[0], in1=dk[1], op=OP.max))
        P.emit("vector", lambda e, dk=dk: e.tensor_tensor(
            out=d2scr, in0=dk[2], in1=dk[3], op=OP.max))
        P.emit("vector", lambda e: e.tensor_tensor(
            out=mbuf, in0=mbuf, in1=d2scr, op=OP.max))
        for kj in range(4):
            last = kj == 3
            fn = lambda e, kj=kj, dk=dk: e.tensor_tensor(
                out=ebufs[kj], in0=dk[kj], in1=mbuf, op=OP.subtract)
            if last:
                P.inc("vector", s_ve, fn)
            else:
                P.emit("vector", fn)
        VSUB = s_ve.n
        P.wait("scalar", s_ve, VSUB)

        def sexp(e):
            for kj in range(4):
                i = e.activation(out=ebufs[kj], in_=ebufs[kj], func=AF.Exp,
                                 scale=1.0 / 32.0)
            return i
        P.inc("scalar", s_sc, sexp)
        SEXP = s_sc.n
        P.wait("vector", s_sc, SEXP)
        P.emit("vector", lambda e: e.tensor_tensor(
            out=mbuf, in0=ebufs[0], in1=ebufs[1], op=OP.add))
        P.emit("vector", lambda e: e.tensor_tensor(
            out=d2scr, in0=ebufs[2], in1=ebufs[3], op=OP.add))
        P.emit("vector", lambda e: e.tensor_tensor(
            out=mbuf, in0=mbuf, in1=d2scr, op=OP.add))
        P.emit("vector", lambda e: e.reciprocal(rmbuf, mbuf))
        P.emit("vector", lambda e, qi=qi: e.tensor_tensor(
            out=rmbuf, in0=rmbuf,
            in1=masks[:, 16 * qi:16 * (qi + 1)].unsqueeze(2
                ).broadcast_to([128, 16, 16]),
            op=OP.mult))
        for kj in range(4):
            P.emit("vector", lambda e, kj=kj: e.tensor_tensor(
                out=ebufs[kj], in0=ebufs[kj], in1=rmbuf, op=OP.mult))
            off = 16 * (kj - qi + 3)
            fn = lambda e, kj=kj, off=off: e.tensor_tensor(
                out=tapt[:, :, off:off + 16], in0=tapt[:, :, off:off + 16],
                in1=ebufs[kj], op=OP.add)
            if kj == 3:
                P.inc("vector", s_ve, fn)
            else:
                P.emit("vector", fn)
    TAPS = s_ve.n

    P.wait("gpsimd", s_ve, TAPS)
    for c in range(8):
        P.dma("gpsimd", s_wo, v_woutb[c], d_wout[128 * c:128 * (c + 1), :])
    WOUT = s_wo.n

    if SL < 5:
        _finish(nc, P, block_debug=(d_dbg, locals()))
        return nc
    D3t_ev = {}
    for tc in range(NTO):
        pbi = 4 + tc % 2
        if tc == 0:
            P.wait("tensor", s_ve, TAPS)
        if tc >= 2:
            P.wait("tensor", s_ve, D3t_ev[tc - 2])
        P.inc("tensor", s_pe, lambda e, tc=tc, pbi=pbi: e.transpose(
            PBh[pbi][0:112, 0:128], v_taptok[:, 112 * tc:112 * (tc + 1)],
            eyeb[:]))
        TP = s_pe.n
        P.wait("vector", s_pe, TP)
        P.inc("vector", s_ve, lambda e, tc=tc, pbi=pbi: e.tensor_copy(
            v_tapf[0:112, 128 * tc:128 * (tc + 1)], PBh[pbi][0:112, 0:128]))
        D3t_ev[tc] = s_ve.n
    TAPF = s_ve.n

    P.inc("vector", s_ve, lambda e: e.memset(AK[:, 0:8 * TOWN], 0.0))
    AOZ = s_ve.n

    P.wait("tensor", s_ve, TAPF)
    P.wait("tensor", s_eb, EBL)
    D3_ev = {}
    nb3 = 0
    for dlt in range(7):
        dv = dlt - 3
        for c in range(8):
            q = 16 * dlt + 2 * c
            for (toff, tn) in TT_OWN:
                pbi = 6 + nb3 % 2
                if nb3 >= 2:
                    P.wait("tensor", s_ve, D3_ev[nb3 - 2])
                nb3 += 1
                lhsT = AQ[0:112, 64 * q:64 * q + 128]
                P.inc("tensor", s_pe, lambda e, lhsT=lhsT, toff=toff, tn=tn,
                      pbi=pbi: e.matmul(PB[pbi][:, 0:tn], lhsT,
                                        v_tapf[0:112, toff:toff + tn],
                                        start=True, stop=True))
                BMM = s_pe.n
                P.wait("vector", s_pe, BMM)
                if nb3 == 1:
                    P.wait("vector", s_ve, AOZ)
                P.emit("vector", lambda e, c=c, dv=dv, toff=toff, tn=tn, pbi=pbi:
                       e.tensor_tensor(
                           out=tmp2[:, 0:tn], in0=PB[pbi][:, 0:tn],
                           in1=AV[:, LB * c + L0 + dv + toff:
                                  LB * c + L0 + dv + toff + tn],
                           op=OP.mult))
                P.inc("vector", s_ve, lambda e, c=c, toff=toff, tn=tn:
                      e.tensor_tensor(
                          out=AK[:, TOWN * c + toff: TOWN * c + toff + tn],
                          in0=AK[:, TOWN * c + toff: TOWN * c + toff + tn],
                          in1=tmp2[:, 0:tn], op=OP.add))
                D3_ev[nb3 - 1] = s_ve.n
    AO = s_ve.n

    fence()

    if SL < 6:
        _finish(nc, P, block_debug=(d_dbg, locals()))
        return nc
    P.wait("tensor", s_wo, WOUT)
    D4_ev = {}
    nb4 = 0
    for tc in range(NTO):
        for jc in range(2):
            pbi = nb4 % 2
            if nb4 >= 2:
                P.wait("tensor", s_ve, D4_ev[nb4 - 2])
            nb4 += 1

            def smm(e, tc=tc, jc=jc, pbi=pbi):
                for c in range(8):
                    e.matmul(PB[pbi][:, 0:512],
                             AK[:, TOWN * c + 128 * tc: TOWN * c + 128 * (tc + 1)],
                             v_woutb[c][:, 512 * jc:512 * (jc + 1)],
                             start=(c == 0), stop=False)
                return e.matmul(PB[pbi][:, 0:512],
                                countv[0:1, L0 + 128 * tc: L0 + 128 * (tc + 1)],
                                bouts[0:1, 512 * jc:512 * (jc + 1)],
                                start=False, stop=True)
            P.inc("tensor", s_pe, smm)
            SMM = s_pe.n
            P.wait("vector", s_pe, SMM)
            P.inc("vector", s_ve, lambda e, tc=tc, jc=jc, pbi=pbi: e.tensor_copy(
                AQ[:, 1024 * tc + 512 * jc: 1024 * tc + 512 * (jc + 1)],
                PB[pbi][:, 0:512]))
            D4_ev[nb4 - 1] = s_ve.n
    SDONE = s_ve.n

    fence()

    if SL < 7:
        _finish(nc, P, block_debug=(d_dbg, locals()))
        return nc
    E0gt_pe = {}
    nb5 = 0
    E0_b2ev = {}
    for tc in range(NTO):
        pbi = 2 + tc % 2
        if tc == 0:
            P.wait("tensor", s_ve, GATES)
        P.inc("tensor", s_pe, lambda e, tc=tc, pbi=pbi: e.transpose(
            PB[pbi][0:8, 0:128], gw[:, 8 * tc:8 * (tc + 1)], eyef[:]))
        GTP = s_pe.n
        P.wait("vector", s_pe, GTP)
        if tc >= 1:
            P.wait("vector", s_pe, E0gt_pe[tc - 1])
        P.inc("vector", s_ve, lambda e, pbi=pbi: e.tensor_copy(
            gT[:], PB[pbi][0:8, 0:128]))
        GTEV = s_ve.n
        P.wait("tensor", s_ve, GTEV)
        for jc in range(2):
            pbj = nb5 % 2
            if nb5 >= 2:
                P.wait("tensor", s_ve, E0_b2ev[nb5 - 2])
            nb5 += 1
            P.inc("tensor", s_pe, lambda e, jc=jc, pbj=pbj: e.matmul(
                PB[pbj][:, 0:512], gT[:], b2s[:, 512 * jc:512 * (jc + 1)],
                start=True, stop=True))
            B2MM = s_pe.n
            E0gt_pe[tc] = B2MM
            P.wait("vector", s_pe, B2MM)
            P.inc("vector", s_ve, lambda e, tc=tc, jc=jc, pbj=pbj: e.tensor_copy(
                AV[:, 1024 * tc + 512 * jc: 1024 * tc + 512 * (jc + 1)],
                PB[pbj][:, 0:512]))
            E0_b2ev[nb5 - 1] = s_ve.n

    fence()

    if SL < 8:
        _finish(nc, P, block_debug=(d_dbg, locals()))
        return nc
    GACC_HALF_V = [None]
    BANK_EV = [s_ve.n] * 8
    w2_pre = {}
    w1done = []
    w1_pre = {}

    def emit_w2_load(e_, jc, fc, widx):
        if widx >= 4:
            P.wait("gpsimd", s_pe, w2done[widx - 4])
        P.dma("gpsimd", s_w2[widx % 4], v_w2buf[widx % 4],
              d_w2[e_, 128 * fc:128 * (fc + 1), 512 * jc:512 * (jc + 1)])
        return s_w2[widx % 4].n

    def emit_w1_group(e_, fg):
        gidx = 16 * (2 * 0 + 0)
        gi = emit_w1_group.count
        emit_w1_group.count += 1
        if gi >= 2:
            P.wait("gpsimd", s_pe, w1done[gi - 2])
        for dc in range(8):
            P.dma("gpsimd", s_w1[fg % 2], wgrp(fg % 2, dc),
                  d_w1[e_, 128 * dc:128 * (dc + 1), 512 * fg:512 * (fg + 1)])
        return s_w1[fg % 2].n
    emit_w1_group.count = 0
    gelu_seq = []
    w2done = []
    P2_EV_PREV = 0
    PH2_PE_PREV = 0
    for e_ in range(NE):
        for tb in range(2):
            P.wait("scalar", s_pe, PH2_PE_PREV)
            nb0 = len(gelu_seq)
            for fg in range(8):
                if (e_, tb, fg) in w1_pre:
                    W1L = w1_pre[(e_, tb, fg)]
                else:
                    W1L = emit_w1_group(e_, fg)

                P.wait("tensor", s_w1[fg % 2], W1L)
                if fg == 5:
                    for pfc in range(4):
                        w2_pre[(e_, tb, 0, pfc)] = emit_w2_load(
                            e_, 0, pfc, len(w2done) + pfc)
                for fsub in range(4):
                    fc = 4 * fg + fsub
                    for th in range(2):
                        nbank = len(gelu_seq)
                        pbi = nbank % 4
                        if nbank - nb0 < 4:
                            P.wait("tensor", s_ve, BANK_EV[pbi])
                        if nbank >= 4:
                            P.wait("tensor", s_sc, gelu_seq[nbank - 4])

                        def hmm(e, e_=e_, tb=tb, fg=fg, fsub=fsub, th=th, pbi=pbi):
                            for dc in range(8):
                                i = e.matmul(
                                    PB[pbi][:, 0:512],
                                    wgrp(fg % 2, dc)[:, 128 * fsub:128 * (fsub + 1)],
                                    XF[:, LB * dc + L0 + 1024 * tb + 512 * th:
                                       LB * dc + L0 + 1024 * tb + 512 * (th + 1)],
                                    start=(dc == 0), stop=(dc == 7))
                            return i
                        P.inc("tensor", s_pe, hmm)
                        HMM = s_pe.n
                        P.wait("scalar", s_pe, HMM)
                        P.inc("scalar", s_sc, lambda e, e_=e_, fc=fc, th=th,
                              pbi=pbi: e.activation(
                                  out=h_view(fc, 512 * th, 512 * (th + 1)),
                                  in_=PB[pbi][:, 0:512], func=AF.Gelu,
                                  bias=b1s[:, 32 * e_ + fc:32 * e_ + fc + 1]))
                        gelu_seq.append(s_sc.n)
                w1done.append(s_pe.n)
            H_READY = s_sc.n

            for jc in range(2):
                for fc in range(32):
                    widx = len(w2done)
                    if (e_, tb, jc, fc) in w2_pre:
                        W2L = w2_pre[(e_, tb, jc, fc)]
                    else:
                        W2L = emit_w2_load(e_, jc, fc, widx)
                    if jc == 1 and fc in (16, 20) and (e_, tb) != (NE - 1, 1):
                        ne_, ntb = (e_, 1) if tb == 0 else (e_ + 1, 0)
                        nfg = 0 if fc == 16 else 1
                        w1_pre[(ne_, ntb, nfg)] = emit_w1_group(ne_, nfg)
                    P.wait("tensor", s_w2[widx % 4], W2L)
                    if fc == 0:
                        P.wait("tensor", s_sc, H_READY)
                        for tcc in range(8):
                            P.wait("tensor", s_ve, BANK_EV[tcc])

                            def mm0(e, jc=jc, tcc=tcc, widx=widx):
                                return e.matmul(
                                    PB[tcc][:, 0:512],
                                    h_view(0, 128 * tcc, 128 * (tcc + 1)),
                                    v_w2buf[widx % 4][:, 0:512],
                                    start=True, stop=False)
                            if tcc == 7:
                                P.inc("tensor", s_pe, mm0)
                            else:
                                P.emit("tensor", lambda e, f=mm0: f(e))
                        w2done.append(s_pe.n)
                        continue

                    def omm(e, jc=jc, fc=fc, widx=widx):
                        for tcc in range(8):
                            i = e.matmul(
                                PB[tcc][:, 0:512],
                                h_view(fc, 128 * tcc, 128 * (tcc + 1)),
                                v_w2buf[widx % 4][:, 0:512],
                                start=False, stop=(fc == 31))
                        return i
                    P.inc("tensor", s_pe, omm)
                    w2done.append(s_pe.n)
                OMM = s_pe.n
                P.wait("vector", s_pe, OMM)
                for tcc in range(8):
                    tcg = 8 * tb + tcc
                    P.emit("vector", lambda e, e_=e_, tcg=tcg, tcc=tcc:
                           e.tensor_scalar(
                               out=tmp2[:, 0:512], in0=PB[tcc][:, 0:512],
                               scalar1=gw[:, 8 * tcg + e_:8 * tcg + e_ + 1],
                               scalar2=None, op0=OP.mult))
                    P.inc("vector", s_ve, lambda e, tcg=tcg, jc=jc:
                          e.tensor_tensor(
                              out=AV[:, 1024 * tcg + 512 * jc:
                                     1024 * tcg + 512 * (jc + 1)],
                              in0=AV[:, 1024 * tcg + 512 * jc:
                                     1024 * tcg + 512 * (jc + 1)],
                              in1=tmp2[:, 0:512], op=OP.add))
                    BANK_EV[tcc] = s_ve.n
                P2_EV_PREV = s_ve.n
                if (e_, tb, jc) == (NE - 1, 0, 1):
                    GACC_HALF_V[0] = s_ve.n
            PH2_PE_PREV = s_pe.n
    GACC = s_ve.n
    GACC_HALF = GACC

    fence()

    if SL < 9:
        _finish(nc, P, block_debug=(d_dbg, locals()))
        return nc
    F_ev, F_mul = {}, {}
    for tc in range(NTO):
        bb = tc % 2
        ob = tc % 3
        if tc >= 2:
            P.wait("sync", s_ve, F_ev[tc - 2])
        P.dma("sync", s_x2[bb], v_xt2[bb][:],
              d_xh[L0 + 128 * tc: L0 + 128 * (tc + 1), :])
        X2 = s_x2[bb].n
        if tc == 0:
            P.wait("scalar", s_ve, GACC)
        if tc >= 2:
            P.wait("scalar", s_ve, F_mul[tc - 2])
        P.inc("scalar", s_sc, lambda e, tc=tc, bb=bb: e.activation(
            out=v_sg[bb][:], in_=AV[:, 1024 * tc:1024 * (tc + 1)], func=AF.Silu))
        SGD = s_sc.n
        P.wait("vector", s_sc, SGD)
        P.wait("vector", s_x2[bb], X2)
        if tc == 0:
            P.wait("vector", s_ve, SDONE)
        if tc >= 3:
            P.wait("vector", s_st[ob], s_st[ob].n)
        P.inc("vector", s_ve, lambda e, tc=tc, bb=bb, ob=ob: e.tensor_tensor(
            out=v_outb[ob][:], in0=v_sg[bb][:],
            in1=AQ[:, 1024 * tc:1024 * (tc + 1)], op=OP.mult))
        F_mul[tc] = s_ve.n
        P.inc("vector", s_ve, lambda e, bb=bb, ob=ob: e.tensor_tensor(
            out=v_outb[ob][:], in0=v_outb[ob][:], in1=v_xt2[bb][:], op=OP.add))
        F_ev[tc] = s_ve.n
        P.wait("sync", s_ve, F_ev[tc])
        P.dma("sync", s_st[ob], d_out[128 * tc:128 * (tc + 1), :], v_outb[ob][:])

    if d_dbg:
        ctx = dict(XF=XF, AQ=AQ, AK=AK, AV=AV, H2=H2, WS=WS, glog=glog, gw=gw,
                   v_ddf=v_ddf, v_ddtok=v_ddtok, v_taptok=v_taptok,
                   v_tapf=v_tapf, STAT=STAT, sbc=sbc, H2f=H2f)
        P.wait("sync", s_ve, s_ve.n)
        P.wait("sync", s_sc, s_sc.n)
        for name, ap in d_dbg.items():
            P.dma("sync", s_st[0], ap, DBG_SRC[name](ctx))

    for _s in s_st:
        P.wait("sync", _s, _s.n)

    with nc.allow_low_precision("bf16 accumulators: 4-term softmax/tap sums, "
                                "ao 7-tap sum, g_acc 2-expert sum (tolerance "
                                "validated)"):
        with nc.Block() as block:
            P.run(block)
    return nc


def _host_prep(L0_attn_x, L0_attn_w_qkv, L0_attn_w_out, L0_attn_b_out,
               L0_ffn_centroids, L0_ffn_W1, L0_ffn_b1, L0_ffn_W2, L0_ffn_b2):
    bf = ml_dtypes.bfloat16
    x = np.ascontiguousarray(np.asarray(L0_attn_x, dtype=np.float32))
    wqkv = np.ascontiguousarray(np.asarray(L0_attn_w_qkv, dtype=np.float32))
    wout = np.ascontiguousarray(np.asarray(L0_attn_w_out, dtype=np.float32))
    bout = np.ascontiguousarray(np.asarray(L0_attn_b_out, dtype=np.float32)
                                .reshape(1, D))
    cent = np.asarray(L0_ffn_centroids, dtype=np.float32)
    ctt = np.ascontiguousarray(cent.T)
    w1 = np.ascontiguousarray(np.asarray(L0_ffn_W1, dtype=np.float32))
    b1 = np.asarray(L0_ffn_b1, dtype=np.float32)
    b1t = np.ascontiguousarray(
        b1.reshape(NE, 32, 128).transpose(2, 0, 1).reshape(128, NE * 32))
    w2 = np.ascontiguousarray(np.asarray(L0_ffn_W2, dtype=np.float32))
    b2 = np.ascontiguousarray(np.asarray(L0_ffn_b2, dtype=np.float32))

    eyeb = np.eye(128, dtype=bf)
    eyef = np.eye(128, dtype=np.float32)
    bonesA = np.zeros((128, 256), dtype=np.float32)
    bonesB = np.zeros((128, 256), dtype=np.float32)
    for c in range(8):
        for hh in range(16):
            if hh // 2 == c:
                rows = slice(64 * (hh % 2), 64 * (hh % 2 + 1))
                bonesA[rows, 32 * c + hh] = 1.0
                bonesB[rows, 32 * c + 16 + hh] = 1.0
    bonesA = bonesA.astype(bf)
    bonesB = bonesB.astype(bf)
    eb = np.zeros((128, 7168), dtype=np.float32)
    for kk in range(112):
        eb[kk, 64 * kk:64 * kk + 64] = 1.0
    eb = eb.astype(bf)
    eone = np.zeros((128, 224), dtype=np.float32)
    for kk in range(112):
        eone[kk, 2 * kk] = 1.0
        if 2 * kk + 1 < 224:
            eone[kk, 2 * kk + 1] = 1.0
    eone = eone.astype(bf)

    wmax = (SEQ - 4) // 2
    in_maps = []
    for core in range(NC):
        bb, j = divmod(core, 4)
        t0 = TOWN * j
        lo, hi = t0 - L0, t0 + TOWN + L0
        xs = x[bb, max(0, lo):min(SEQ, hi)]
        xh = np.zeros((LB, D), dtype=np.float32)
        xh[max(0, -lo): LB - max(0, hi - SEQ)] = xs
        tg = t0 + np.arange(-L0, TOWN + L0)
        wlo = np.maximum(np.ceil((tg - 3) / 2.0), 0)
        whi = np.minimum(np.floor(tg / 2.0), wmax)
        cv = np.maximum(whi - wlo + 1, 0).astype(np.float32)
        countv = cv.reshape(1, LB)[:, :2176].astype(bf)
        masks = np.zeros((128, 64), dtype=np.float32)
        for qi in range(4):
            for tc in range(16):
                t = t0 + 128 * tc + np.arange(128)
                ok = (t % 2) == (qi % 2)
                w = (t - qi) // 2
                ok &= (w >= 0) & (w <= wmax)
                masks[:, 16 * qi + tc] = ok.astype(np.float32)
        desc8 = np.tile((8.0 - np.arange(8, dtype=np.float32)), 16)
        desc8 = np.tile(desc8.reshape(1, 128), (128, 1))
        in_maps.append({
            "xh": xh, "wqkv": wqkv, "wout": wout, "bout": bout, "ctt": ctt,
            "desc8": desc8,
            "w1": w1, "b1t": b1t, "w2": w2, "b2": b2,
            "eyeb": eyeb, "eyef": eyef, "bonesA": bonesA, "bonesB": bonesB,
            "eone": eone, "eb": eb,
            "masks": masks, "countv": countv,
        })
    return in_maps


def kernel(L0_attn_x, L0_attn_w_qkv, L0_attn_w_out, L0_attn_b_out,
           L0_ffn_centroids, L0_ffn_W1, L0_ffn_b1, L0_ffn_W2, L0_ffn_b2):
    in_maps = _host_prep(L0_attn_x, L0_attn_w_qkv, L0_attn_w_out, L0_attn_b_out,
                         L0_ffn_centroids, L0_ffn_W1, L0_ffn_b1, L0_ffn_W2,
                         L0_ffn_b2)
    nc = build_program(DBG.items())
    res = run_bass_kernel_spmd(nc, in_maps, core_ids=list(range(NC)))
    out = np.empty((B, SEQ, D), dtype=np.float32)
    for core in range(NC):
        bb, j = divmod(core, 4)
        out[bb, TOWN * j:TOWN * (j + 1)] = res.results[core]["out"]
    if DBG:
        kernel.dbg = [{k: r["dbg_" + k] for k in dict(DBG)}
                      for r in res.results]
    return out


# version 12
# speedup vs baseline: 1.1432x; 1.0043x over previous
import numpy as np
import ml_dtypes
import concourse.bass as bass
import concourse.mybir as mybir
from concourse.bass_utils import run_bass_kernel_spmd
from contextlib import ExitStack

F32 = mybir.dt.float32
BF16 = mybir.dt.bfloat16
AF = mybir.ActivationFunctionType
OP = mybir.AluOpType
AX = mybir.AxisListType

D = 1024
DFF = 4096
NE = 8
SEQ = 8192
B = 2
TOWN = 2048
L0 = 128
LB = 2304
NT = LB // 128
NTO = TOWN // 128
NC = 8
EPS = 1e-5

TT_FULL = [(0, 512), (512, 512), (1024, 512), (1536, 512), (2048, 256)]
TT_OWN = [(0, 512), (512, 512), (1024, 512), (1536, 512)]

ENGS = ("sync", "gpsimd", "scalar", "vector", "tensor")

DBG = {}
DBG_SRC = {}
STAGE_LIMIT = 99


class Sem:
    def __init__(self, nc, es, name):
        self.h = es.enter_context(nc.semaphore(name))
        self.n = 0


class Prog:

    def __init__(self):
        self.streams = {e: [] for e in ENGS}

    def emit(self, eng, fn):
        self.streams[eng].append(fn)

    def wait(self, eng, sem, val):
        h = sem.h
        self.emit(eng, lambda e: e.wait_ge(h, val))

    def inc(self, eng, sem, buildf, k=1):
        sem.n += k
        h = sem.h
        self.emit(eng, lambda e: buildf(e).then_inc(h, k))

    def dma(self, eng, sem, out, in_):
        sem.n += 16
        h = sem.h
        self.emit(eng, lambda e: e.dma_start(out=out, in_=in_).then_inc(h, 16))

    def run(self, block):
        for eng in ENGS:
            thunks = self.streams[eng]
            def body(e, thunks=thunks):
                for f in thunks:
                    f(e)
            getattr(block, eng)(body)



def _finish(nc, P, block_debug):
    d_dbg, loc = block_debug
    if d_dbg:
        ctx = {k: loc[k] for k in ("XF", "AQ", "AK", "AV", "H2", "glog", "gw",
                                   "v_ddf", "v_ddtok", "v_taptok", "v_tapf",
                                   "STAT", "sbc", "H2f", "WS")
               if k in loc}
        s_ve = loc["s_ve"]; s_sc = loc["s_sc"]; s_pe = loc["s_pe"]
        s_st = loc["s_st"][0]
        P.wait("sync", s_ve, s_ve.n)
        P.wait("sync", s_sc, s_sc.n)
        P.wait("sync", s_pe, s_pe.n)
        for name, ap in d_dbg.items():
            P.dma("sync", s_st, ap, DBG_SRC[name](ctx))
        P.wait("sync", s_st, s_st.n)
    with nc.allow_low_precision("bf16 accumulators (see build_program)"):
        with nc.Block() as block:
            P.run(block)


def build_program(debug_taps=()):
    nc = bass.Bass("TRN2", target_bir_lowering=False)
    es = ExitStack()

    d_xh = nc.dram_tensor("xh", [LB, D], F32, kind="ExternalInput").ap()
    d_wqkv = nc.dram_tensor("wqkv", [D, 3 * D], F32, kind="ExternalInput").ap()
    d_wout = nc.dram_tensor("wout", [D, D], F32, kind="ExternalInput").ap()
    d_bout = nc.dram_tensor("bout", [1, D], F32, kind="ExternalInput").ap()
    d_ctt = nc.dram_tensor("ctt", [D, NE], F32, kind="ExternalInput").ap()
    d_w1 = nc.dram_tensor("w1", [NE, D, DFF], F32, kind="ExternalInput").ap()
    d_b1t = nc.dram_tensor("b1t", [128, NE * 32], F32, kind="ExternalInput").ap()
    d_w2 = nc.dram_tensor("w2", [NE, DFF, D], F32, kind="ExternalInput").ap()
    d_b2 = nc.dram_tensor("b2", [NE, D], F32, kind="ExternalInput").ap()
    d_eyeb = nc.dram_tensor("eyeb", [128, 128], BF16, kind="ExternalInput").ap()
    d_eyef = nc.dram_tensor("eyef", [128, 128], F32, kind="ExternalInput").ap()
    d_bonesA = nc.dram_tensor("bonesA", [128, 256], BF16, kind="ExternalInput").ap()
    d_bonesB = nc.dram_tensor("bonesB", [128, 256], BF16, kind="ExternalInput").ap()
    d_eone = nc.dram_tensor("eone", [128, 224], BF16, kind="ExternalInput").ap()
    d_eb = nc.dram_tensor("eb", [128, 7168], BF16, kind="ExternalInput").ap()
    d_masks = nc.dram_tensor("masks", [128, 64], F32, kind="ExternalInput").ap()
    d_desc8 = nc.dram_tensor("desc8", [128, 128], F32, kind="ExternalInput").ap()
    d_countv = nc.dram_tensor("countv", [1, 2176], BF16, kind="ExternalInput").ap()
    d_out = nc.dram_tensor("out", [TOWN, D], F32, kind="ExternalOutput").ap()
    d_dbg = {}
    for name, (shape, dt) in dict(debug_taps).items():
        d_dbg[name] = nc.dram_tensor("dbg_" + name, list(shape), dt,
                                     kind="ExternalOutput").ap()

    t_XF = es.enter_context(nc.sbuf_tensor("XF", [128, 8 * LB], BF16))
    t_AQ = es.enter_context(nc.sbuf_tensor("AQ", [128, 8 * LB], BF16))
    t_AK = es.enter_context(nc.sbuf_tensor("AK", [128, 8 * LB], BF16))
    t_AV = es.enter_context(nc.sbuf_tensor("AV", [128, 8 * LB], BF16))
    t_H2 = es.enter_context(nc.sbuf_tensor("H2", [128, 14336], BF16))
    t_WS = es.enter_context(nc.sbuf_tensor("WS", [128, 10240], BF16))
    XF, AQ, AK, AV, H2, WS = (t.ap() for t in (t_XF, t_AQ, t_AK, t_AV, t_H2, t_WS))
    AVf = AV.bitcast(F32)
    H2f = H2.bitcast(F32)

    small = {}
    for nm, shape, dt in [
            ("eyeb", [128, 128], BF16), ("eyef", [128, 128], F32),
            ("bonesA", [128, 256], BF16), ("bonesB", [128, 256], BF16),
            ("eone", [128, 224], BF16),
            ("masks", [128, 64], F32), ("countv", [1, 2176], BF16),
            ("desc8", [128, 128], F32),
            ("ctf", [128, 64], F32), ("b1s", [128, NE * 32], F32),
            ("b2s", [NE, D], BF16), ("bouts", [1, D], BF16),
            ("glog", [128, 128], F32), ("gw", [128, 128], F32),
            ("sbc", [128, 8], F32), ("srow", [1, 8], F32),
            ("ones1", [1, 128], F32), ("ones128", [128, 1], F32),
            ("gT", [NE, 128], BF16), ("tmp2", [128, 512], BF16),
            ("epsv", [128, 1], F32),
            ("stat", [128, 16], F32)]:
        small[nm] = es.enter_context(nc.sbuf_tensor("sb_" + nm, shape, dt)).ap()
    eyeb, eyef = small["eyeb"], small["eyef"]
    bonesA, bonesB, eone = small["bonesA"], small["bonesB"], small["eone"]
    masks, countv, ctf, b1s = (small[x] for x in ("masks", "countv", "ctf", "b1s"))
    b2s, bouts, glog, gw = (small[x] for x in ("b2s", "bouts", "glog", "gw"))
    sbc, srow, ones1, ones128 = (small[x] for x in
                                 ("sbc", "srow", "ones1", "ones128"))
    gT, tmp2, STAT = small["gT"], small["tmp2"], small["stat"]

    v_xt = [AVf[:, 0:1024], AVf[:, 1024:2048]]
    v_sq = AVf[:, 2048:3072]
    v_xtr = [[AVf[:, 3072 + bb * 1024 + c * 128: 3072 + bb * 1024 + (c + 1) * 128]
              for c in range(8)] for bb in range(2)]
    v_xnt = [AV[:, 10240:11264], AV[:, 11264:12288]]

    v_P = [H2[:, 0:2048], H2[:, 2048:4096]]
    v_ddf = H2[:, 4608:4608 + 2048]
    v_ddtok = H2[:, 6656:6656 + 1792]
    v_taptok = H2[:, 8448:8448 + 1792]
    v_tapf = H2[:, 10240:10240 + 2048]
    v_sm = [H2[:, 12288 + i * 256: 12288 + (i + 1) * 256] for i in range(8)]
    v_woutb = [H2[:, 1024 * c: 1024 * (c + 1)] for c in range(8)]
    v_lgscr = v_sm[0].bitcast(F32)

    def h_view(fc, c0, c1):
        if fc < 18:
            return AK[:, 1024 * fc + c0: 1024 * fc + c1]
        return H2[:, 1024 * (fc - 18) + c0: 1024 * (fc - 18) + c1]

    AKf = AK.bitcast(F32)
    v_xt2 = [AKf[:, 1024 * i: 1024 * (i + 1)] for i in range(4)]
    v_sg = [H2f[:, 2048:3072], H2f[:, 3072:4096]]
    v_outb = [H2f[:, 4096:5120], H2f[:, 5120:6144], H2f[:, 6144:7168]]
    v_tmpf = H2f[:, 3072:4096]


    def wgrp(bb, dc):
        return WS[:, 4096 * bb + 512 * dc: 4096 * bb + 512 * (dc + 1)]
    v_w2buf = [WS[:, 8192 + 512 * i: 8192 + 512 * (i + 1)] for i in range(4)]

    PB = [es.enter_context(nc.psum_tensor("pb%d" % i, [128, 512], F32)).ap()
          for i in range(8)]
    PBh = [p.bitcast(BF16) for p in PB]

    SL = STAGE_LIMIT
    P = Prog()
    s_x = [Sem(nc, es, "s_x0"), Sem(nc, es, "s_x1")]
    s_c = Sem(nc, es, "s_c")
    s_wq = [Sem(nc, es, "s_wq0"), Sem(nc, es, "s_wq1")]
    s_wo = Sem(nc, es, "s_wo")
    s_eb = Sem(nc, es, "s_eb")
    s_w1 = [Sem(nc, es, "s_w1a"), Sem(nc, es, "s_w1b")]
    s_w2 = [Sem(nc, es, "s_w2%d" % i) for i in range(4)]
    s_sc = Sem(nc, es, "s_sc")
    s_ve = Sem(nc, es, "s_ve")
    s_pe = Sem(nc, es, "s_pe")
    s_st = [Sem(nc, es, "s_st%d" % i) for i in range(3)]
    s_x2 = [Sem(nc, es, "s_x2a"), Sem(nc, es, "s_x2b")]

    def fence(engs=("tensor", "vector", "scalar", "sync")):
        vals = [(s_pe, s_pe.n), (s_ve, s_ve.n), (s_sc, s_sc.n)]
        for eng in engs:
            for sem, v in vals:
                P.wait(eng, sem, v)

    s_vb = Sem(nc, es, "s_vb")
    vbs = small["stat"][:, 14:16]

    def vbar():
        P.inc("vector", s_vb, lambda e: e.tensor_copy(vbs[0:1, 0:1],
                                                      vbs[0:1, 1:2]))
        P.wait("vector", s_vb, s_vb.n)

    P.dma("gpsimd", s_c, small["eyeb"][:], d_eyeb[:])
    P.dma("gpsimd", s_c, small["eyef"][:], d_eyef[:])
    P.dma("gpsimd", s_c, bonesA[:], d_bonesA[:])
    P.dma("gpsimd", s_c, bonesB[:], d_bonesB[:])
    P.dma("gpsimd", s_c, eone[:], d_eone[:])
    P.dma("gpsimd", s_c, masks[:], d_masks[:])
    P.dma("gpsimd", s_c, small["desc8"][:], d_desc8[:])
    P.dma("gpsimd", s_c, countv[:], d_countv[:])
    P.dma("gpsimd", s_c, b1s[:], d_b1t[:])
    P.dma("gpsimd", s_c, b2s[:], d_b2[:])
    P.dma("gpsimd", s_c, bouts[:], d_bout[:])
    for dc in range(8):
        P.dma("gpsimd", s_c, ctf[:, 8 * dc:8 * dc + 8],
              d_ctt[128 * dc:128 * (dc + 1), :])
    NCONST = s_c.n

    P.emit("vector", lambda e: e.memset(ones1[:], 1.0))
    P.emit("vector", lambda e: e.memset(small["epsv"][:], EPS))
    P.inc("vector", s_ve, lambda e: e.memset(ones128[:], 1.0))
    VE_ONES = s_ve.n

    P.wait("tensor", s_c, NCONST)
    P.wait("tensor", s_ve, VE_ONES)

    def mm_srow(e):
        for dc in range(8):
            i = e.matmul(PB[6][0:1, 0:8], ones128[:], ctf[:, 8 * dc:8 * dc + 8],
                         start=(dc == 0), stop=(dc == 7))
        return i
    P.inc("tensor", s_pe, mm_srow)
    P.wait("vector", s_pe, s_pe.n)
    P.inc("vector", s_ve, lambda e: e.tensor_copy(srow[:], PB[6][0:1, 0:8]))
    P.wait("tensor", s_ve, s_ve.n)
    P.inc("tensor", s_pe, lambda e: e.matmul(PB[7][:, 0:8], ones1[:], srow[:],
                                             start=True, stop=True))
    P.wait("vector", s_pe, s_pe.n)
    P.inc("vector", s_ve, lambda e: e.tensor_copy(sbc[:], PB[7][:, 0:8]))
    VE_SBC = s_ve.n

    A_xn, A_bt, A_gmm, A_glog = {}, {}, {}, {}
    A_bt_ev, A_ft_ev = {}, {}
    for k in range(NT):
        bb = k % 2
        owned = 1 <= k <= NTO
        if k >= 2:
            P.wait("sync", s_ve, A_xn[k - 2])
            P.wait("sync", s_pe, A_gmm.get(k - 2, A_bt.get(k - 2, 0)))
        P.dma("sync", s_x[bb], v_xt[bb][:], d_xh[128 * k:128 * (k + 1), :])
        XLD = s_x[bb].n

        P.wait("scalar", s_x[bb], XLD)
        P.emit("scalar", lambda e, bb=bb: e.activation(
            out=v_sq[:], in_=v_xt[bb][:], func=AF.Square,
            accum_out=STAT[:, 2 + bb:3 + bb]))
        P.inc("scalar", s_sc, lambda e, bb=bb: e.activation(
            out=v_sq[:], in_=v_xt[bb][:], func=AF.Copy,
            accum_out=STAT[:, 0 + bb:1 + bb]))
        ASTAT = s_sc.n

        P.wait("vector", s_sc, ASTAT)
        P.emit("vector", lambda e, bb=bb: e.tensor_scalar(
            out=STAT[:, 6 + bb:7 + bb], in0=STAT[:, 2 + bb:3 + bb],
            scalar1=1.0 / D, scalar2=None, op0=OP.mult))
        P.emit("vector", lambda e, bb=bb: e.tensor_scalar(
            out=STAT[:, 4 + bb:5 + bb], in0=STAT[:, 0 + bb:1 + bb],
            scalar1=1.0 / D, scalar2=None, op0=OP.mult))
        vbar()
        P.inc("vector", s_ve, lambda e, bb=bb: e.scalar_tensor_tensor(
            out=STAT[:, 6 + bb:7 + bb], in0=STAT[:, 4 + bb:5 + bb],
            scalar=STAT[:, 4 + bb:5 + bb], in1=STAT[:, 6 + bb:7 + bb],
            op0=OP.mult, op1=OP.subtract))
        AVAR = s_ve.n
        P.wait("scalar", s_ve, AVAR)
        P.inc("scalar", s_sc, lambda e, bb=bb: e.activation(
            out=STAT[:, 8 + bb:9 + bb], in_=STAT[:, 6 + bb:7 + bb],
            func=AF.Sqrt, bias=small["epsv"][:], scale=-1.0))
        ASD = s_sc.n
        P.wait("vector", s_sc, ASD)
        P.emit("vector", lambda e, bb=bb: e.reciprocal(
            STAT[:, 10 + bb:11 + bb], STAT[:, 8 + bb:9 + bb]))
        vbar()
        P.inc("vector", s_ve, lambda e, bb=bb: e.tensor_scalar(
            out=v_xnt[bb][:], in0=v_xt[bb][:],
            scalar1=STAT[:, 4 + bb:5 + bb], scalar2=STAT[:, 10 + bb:11 + bb],
            op0=OP.subtract, op1=OP.mult))
        A_xn[k] = s_ve.n

        P.wait("tensor", s_ve, A_xn[k])
        for c in range(8):
            pbi = 4 + (c % 2)
            prev = A_bt_ev.get((k, c - 2), A_bt_ev.get((k - 1, 6 + (c % 2))))
            if prev:
                P.wait("tensor", s_ve, prev)
            P.inc("tensor", s_pe, lambda e, bb=bb, c=c, pbi=pbi: e.transpose(
                PBh[pbi][:, 0:128], v_xnt[bb][:, 128 * c:128 * (c + 1)], eyeb[:]))
            A_bt[k] = s_pe.n
            P.wait("vector", s_pe, A_bt[k])
            P.inc("vector", s_ve, lambda e, k=k, c=c, pbi=pbi: e.tensor_copy(
                XF[:, LB * c + 128 * k: LB * c + 128 * (k + 1)],
                PBh[pbi][:, 0:128]))
            A_bt_ev[(k, c)] = s_ve.n
        if owned:
            for c in range(8):
                pbi = 2 + (c % 2)
                prev = A_ft_ev.get((k, c - 2), A_ft_ev.get((k - 1, 6 + (c % 2))))
                if prev:
                    P.wait("tensor", s_ve, prev)
                P.inc("tensor", s_pe, lambda e, bb=bb, c=c, pbi=pbi: e.transpose(
                    PB[pbi][:, 0:128], v_xt[bb][:, 128 * c:128 * (c + 1)], eyef[:]))
                AFT = s_pe.n
                P.wait("vector", s_pe, AFT)
                P.inc("vector", s_ve, lambda e, bb=bb, c=c, pbi=pbi: e.tensor_copy(
                    v_xtr[bb][c][:], PB[pbi][:, 0:128]))
                A_ft_ev[(k, c)] = s_ve.n
            gb = (k - 1) % 2
            P.wait("tensor", s_ve, A_ft_ev[(k, 7)])
            if k >= 3:
                P.wait("tensor", s_ve, A_glog[k - 2])

            def gmm(e, bb=bb, gb=gb):
                for dc in range(8):
                    i = e.matmul(PB[gb][:, 0:8], v_xtr[bb][dc][:],
                                 ctf[:, 8 * dc:8 * dc + 8],
                                 start=(dc == 0), stop=(dc == 7))
                return i
            P.inc("tensor", s_pe, gmm)
            A_gmm[k] = s_pe.n
            P.wait("vector", s_pe, A_gmm[k])
            if k == 1:
                P.wait("vector", s_ve, VE_SBC)
            P.emit("vector", lambda e, bb=bb, gb=gb: e.scalar_tensor_tensor(
                out=v_lgscr[:, 0:8], in0=sbc[:],
                scalar=STAT[:, 4 + bb:5 + bb], in1=PB[gb][:, 0:8],
                op0=OP.mult, op1=OP.subtract))
            vbar()
            P.inc("vector", s_ve, lambda e, k=k, bb=bb: e.tensor_scalar(
                out=glog[:, 8 * (k - 1):8 * k], in0=v_lgscr[:, 0:8],
                scalar1=STAT[:, 10 + bb:11 + bb], scalar2=-1.0,
                op0=OP.mult, op1=OP.mult))
            A_glog[k] = s_ve.n

    P.wait("scalar", s_ve, A_glog[NTO])
    P.inc("scalar", s_sc, lambda e: e.activation(out=v_lgscr[:, 0:128],
                                                 in_=glog[:], func=AF.Exp,
                                                 scale=-1.0))
    SIG = s_sc.n
    P.wait("vector", s_sc, SIG)
    P.emit("vector", lambda e: e.tensor_scalar(
        out=v_lgscr[:, 0:128], in0=v_lgscr[:, 0:128], scalar1=1.0,
        scalar2=None, op0=OP.add))
    P.emit("vector", lambda e: e.reciprocal(glog[:], v_lgscr[:, 0:128]))
    gv = glog.rearrange("p (t e) -> p t e", e=8)
    gwv = gw.rearrange("p (t e) -> p t e", e=8)
    mmf = v_sm[1].bitcast(F32)
    m1 = mmf[:, 0:16]
    t1 = mmf[:, 16:32]
    m2v = mmf[:, 32:48]
    t2 = mmf[:, 48:64]
    den = mmf[:, 64:80]
    rden = mmf[:, 80:96]
    scr = v_sm[2].bitcast(F32)[:, 0:128].rearrange("p (t e) -> p t e", e=8)
    scr2 = v_sm[3].bitcast(F32)[:, 0:128].rearrange("p (t e) -> p t e", e=8)
    descv = small["desc8"].rearrange("p (t e) -> p t e", e=8)

    def bc16(ap):
        return ap.unsqueeze(2).broadcast_to([128, 16, 8])

    P.emit("vector", lambda e: e.tensor_reduce(out=m1[:], in_=gv, axis=AX.X,
                                               op=OP.max))
    vbar()
    P.emit("vector", lambda e: e.tensor_tensor(out=scr, in0=gv, in1=bc16(m1),
                                               op=OP.is_equal))
    P.emit("vector", lambda e: e.tensor_tensor(out=scr, in0=scr, in1=descv,
                                               op=OP.mult))
    vbar()
    P.emit("vector", lambda e: e.tensor_reduce(out=t1[:], in_=scr, axis=AX.X,
                                               op=OP.max))
    vbar()
    P.emit("vector", lambda e: e.tensor_tensor(out=scr, in0=scr, in1=bc16(t1),
                                               op=OP.is_equal))
    vbar()
    P.emit("vector", lambda e: e.scalar_tensor_tensor(
        out=scr2, in0=scr, scalar=-1e30, in1=gv, op0=OP.mult, op1=OP.add))
    vbar()
    P.emit("vector", lambda e: e.tensor_reduce(out=m2v[:], in_=scr2, axis=AX.X,
                                               op=OP.max))
    vbar()
    P.emit("vector", lambda e: e.tensor_tensor(out=scr2, in0=scr2, in1=bc16(m2v),
                                               op=OP.is_equal))
    P.emit("vector", lambda e: e.tensor_tensor(out=scr2, in0=scr2, in1=descv,
                                               op=OP.mult))
    vbar()
    P.emit("vector", lambda e: e.tensor_reduce(out=t2[:], in_=scr2, axis=AX.X,
                                               op=OP.max))
    vbar()
    P.emit("vector", lambda e: e.tensor_tensor(out=scr2, in0=scr2, in1=bc16(t2),
                                               op=OP.is_equal))
    vbar()
    P.emit("vector", lambda e: e.tensor_tensor(out=scr, in0=scr, in1=scr2,
                                               op=OP.add))
    P.emit("vector", lambda e: e.tensor_tensor(out=den[:], in0=m1[:], in1=m2v[:],
                                               op=OP.add))
    vbar()
    P.emit("vector", lambda e: e.reciprocal(rden[:], den[:]))
    vbar()
    P.emit("vector", lambda e: e.tensor_tensor(out=gwv, in0=gv, in1=scr,
                                               op=OP.mult))
    P.inc("vector", s_ve, lambda e: e.tensor_tensor(
        out=gwv, in0=gwv, in1=bc16(rden), op=OP.mult))
    GATES = s_ve.n

    fence()

    if SL < 2:
        _finish(nc, P, block_debug=(d_dbg, locals()))
        return nc
    C_grp_pe = {}
    C_ev = []
    ngC = 0
    for jg in range(6):
        if jg >= 2:
            P.wait("gpsimd", s_pe, C_grp_pe[jg - 2])
        for dc in range(8):
            P.dma("gpsimd", s_wq[jg % 2], wgrp(jg % 2, dc),
                  d_wqkv[128 * dc:128 * (dc + 1), 512 * jg:512 * (jg + 1)])
        WQL = s_wq[jg % 2].n

        P.wait("tensor", s_wq[jg % 2], WQL)
        for jci in range(4):
            j = 512 * jg + 128 * jci
            for (toff, tn) in TT_FULL:
                pbi = ngC % 4
                if ngC >= 4:
                    P.wait("tensor", s_ve, C_ev[ngC - 4])
                ngC += 1

                def qmm(e, jg=jg, jci=jci, toff=toff, tn=tn, pbi=pbi):
                    for dc in range(8):
                        i = e.matmul(
                            PB[pbi][:, 0:tn],
                            wgrp(jg % 2, dc)[:, 128 * jci:128 * (jci + 1)],
                            XF[:, LB * dc + toff: LB * dc + toff + tn],
                            start=(dc == 0), stop=(dc == 7))
                    return i
                P.inc("tensor", s_pe, qmm)
                MM = s_pe.n
                C_grp_pe[jg] = MM
                c = (j % 1024) // 128
                dst = (AQ, AK, AV)[j // 1024]
                P.wait("vector", s_pe, MM)
                P.inc("vector", s_ve, lambda e, dst=dst, c=c, toff=toff, tn=tn,
                      pbi=pbi: e.tensor_copy(
                          dst[:, LB * c + toff: LB * c + toff + tn],
                          PB[pbi][:, 0:tn]))
                C_ev.append(s_ve.n)

    fence()

    if SL < 3:
        _finish(nc, P, block_debug=(d_dbg, locals()))
        return nc
    D1_pe, D1_ev = {}, {}
    for dp in range(4):
        dls = [2 * dp] + ([2 * dp + 1] if dp < 3 else [])
        for c in range(8):
            prev = D1_pe.get((dp, c - 1), D1_pe.get((dp - 1, 7)))
            if prev:
                P.wait("vector", s_pe, prev)
            for li, dlt in enumerate(dls):
                dv = dlt - 3
                P.inc("vector", s_ve, lambda e, c=c, dv=dv, li=li: e.tensor_tensor(
                    out=v_P[li][:, 0:TOWN],
                    in0=AQ[:, LB * c + L0: LB * c + L0 + TOWN],
                    in1=AK[:, LB * c + L0 + dv: LB * c + L0 + dv + TOWN],
                    op=OP.mult))
            PDONE = s_ve.n
            P.wait("tensor", s_ve, PDONE)
            if c == 0 and dp > 0:
                P.wait("tensor", s_ve, D1_ev[dp - 1])

            def ddmm(e, c=c, dls=dls):
                for tt in range(4):
                    for li, dlt in enumerate(dls):
                        bm = (bonesA, bonesB)[li]
                        i = e.matmul(PB[tt][0:32, 0:512],
                                     bm[:, 32 * c:32 * (c + 1)],
                                     v_P[li][:, 512 * tt:512 * (tt + 1)],
                                     start=(c == 0 and li == 0),
                                     stop=(c == 7 and li == len(dls) - 1))
                return i
            P.inc("tensor", s_pe, ddmm)
            D1_pe[(dp, c)] = s_pe.n
        P.wait("vector", s_pe, D1_pe[(dp, 7)])
        nrow = 16 * len(dls)
        for tt in range(4):
            P.inc("vector", s_ve, lambda e, dp=dp, tt=tt, nrow=nrow: e.tensor_copy(
                v_ddf[32 * dp:32 * dp + nrow, 512 * tt:512 * (tt + 1)],
                PB[tt][0:nrow, 0:512]))
        D1_ev[dp] = s_ve.n
    K_DONE = s_pe.n

    if SL < 4:
        _finish(nc, P, block_debug=(d_dbg, locals()))
        return nc
    P.wait("gpsimd", s_ve, s_ve.n)
    P.dma("gpsimd", s_eb, AQ[:, 0:7168], d_eb[:])
    EBL = s_eb.n

    D15_ev = {}
    for tc in range(NTO):
        pbi = 4 + tc % 2
        if tc >= 2:
            P.wait("tensor", s_ve, D15_ev[tc - 2])
        if tc < 2:
            P.wait("tensor", s_ve, D1_ev[3])
        P.inc("tensor", s_pe, lambda e, tc=tc, pbi=pbi: e.transpose(
            PBh[pbi][:, 0:128], v_ddf[0:112, 128 * tc:128 * (tc + 1)],
            eyeb[0:112, :]))
        TP = s_pe.n
        P.wait("vector", s_pe, TP)
        P.inc("vector", s_ve, lambda e, tc=tc, pbi=pbi: e.tensor_copy(
            v_ddtok[:, 112 * tc:112 * (tc + 1)], PBh[pbi][:, 0:112]))
        D15_ev[tc] = s_ve.n

    P.inc("vector", s_ve, lambda e: e.memset(v_taptok[:], 0.0))
    ddt = v_ddtok.rearrange("p (t x) -> p t x", x=112)
    tapt = v_taptok.rearrange("p (t x) -> p t x", x=112)
    ebufs = [v_sm[i].rearrange("p (t h) -> p t h", h=16) for i in range(4)]
    mbuf = v_sm[4].rearrange("p (t h) -> p t h", h=16)
    rmbuf = v_sm[5].rearrange("p (t h) -> p t h", h=16)
    d2scr = v_sm[6].rearrange("p (t h) -> p t h", h=16)

    for qi in range(4):
        dk = [ddt[:, :, 16 * (kj - qi + 3): 16 * (kj - qi + 3) + 16]
              for kj in range(4)]
        P.emit("vector", lambda e, dk=dk: e.tensor_tensor(
            out=mbuf, in0=dk[0], in1=dk[1], op=OP.max))
        P.emit("vector", lambda e, dk=dk: e.tensor_tensor(
            out=d2scr, in0=dk[2], in1=dk[3], op=OP.max))
        P.emit("vector", lambda e: e.tensor_tensor(
            out=mbuf, in0=mbuf, in1=d2scr, op=OP.max))
        for kj in range(4):
            last = kj == 3
            fn = lambda e, kj=kj, dk=dk: e.tensor_tensor(
                out=ebufs[kj], in0=dk[kj], in1=mbuf, op=OP.subtract)
            if last:
                P.inc("vector", s_ve, fn)
            else:
                P.emit("vector", fn)
        VSUB = s_ve.n
        P.wait("scalar", s_ve, VSUB)

        def sexp(e):
            for kj in range(4):
                i = e.activation(out=ebufs[kj], in_=ebufs[kj], func=AF.Exp,
                                 scale=1.0 / 32.0)
            return i
        P.inc("scalar", s_sc, sexp)
        SEXP = s_sc.n
        P.wait("vector", s_sc, SEXP)
        P.emit("vector", lambda e: e.tensor_tensor(
            out=mbuf, in0=ebufs[0], in1=ebufs[1], op=OP.add))
        P.emit("vector", lambda e: e.tensor_tensor(
            out=d2scr, in0=ebufs[2], in1=ebufs[3], op=OP.add))
        P.emit("vector", lambda e: e.tensor_tensor(
            out=mbuf, in0=mbuf, in1=d2scr, op=OP.add))
        P.emit("vector", lambda e: e.reciprocal(rmbuf, mbuf))
        P.emit("vector", lambda e, qi=qi: e.tensor_tensor(
            out=rmbuf, in0=rmbuf,
            in1=masks[:, 16 * qi:16 * (qi + 1)].unsqueeze(2
                ).broadcast_to([128, 16, 16]),
            op=OP.mult))
        for kj in range(4):
            P.emit("vector", lambda e, kj=kj: e.tensor_tensor(
                out=ebufs[kj], in0=ebufs[kj], in1=rmbuf, op=OP.mult))
            off = 16 * (kj - qi + 3)
            fn = lambda e, kj=kj, off=off: e.tensor_tensor(
                out=tapt[:, :, off:off + 16], in0=tapt[:, :, off:off + 16],
                in1=ebufs[kj], op=OP.add)
            if kj == 3:
                P.inc("vector", s_ve, fn)
            else:
                P.emit("vector", fn)
    TAPS = s_ve.n

    P.wait("gpsimd", s_ve, TAPS)
    for c in range(8):
        P.dma("gpsimd", s_wo, v_woutb[c], d_wout[128 * c:128 * (c + 1), :])
    WOUT = s_wo.n

    if SL < 5:
        _finish(nc, P, block_debug=(d_dbg, locals()))
        return nc
    D3t_ev = {}
    for tc in range(NTO):
        pbi = 4 + tc % 2
        if tc == 0:
            P.wait("tensor", s_ve, TAPS)
        if tc >= 2:
            P.wait("tensor", s_ve, D3t_ev[tc - 2])
        P.inc("tensor", s_pe, lambda e, tc=tc, pbi=pbi: e.transpose(
            PBh[pbi][0:112, 0:128], v_taptok[:, 112 * tc:112 * (tc + 1)],
            eyeb[:]))
        TP = s_pe.n
        P.wait("vector", s_pe, TP)
        P.inc("vector", s_ve, lambda e, tc=tc, pbi=pbi: e.tensor_copy(
            v_tapf[0:112, 128 * tc:128 * (tc + 1)], PBh[pbi][0:112, 0:128]))
        D3t_ev[tc] = s_ve.n
    TAPF = s_ve.n

    P.inc("vector", s_ve, lambda e: e.memset(AK[:, 0:8 * TOWN], 0.0))
    AOZ = s_ve.n

    P.wait("tensor", s_ve, TAPF)
    P.wait("tensor", s_eb, EBL)
    D3_ev = {}
    nb3 = 0
    for dlt in range(7):
        dv = dlt - 3
        for c in range(8):
            q = 16 * dlt + 2 * c
            for (toff, tn) in TT_OWN:
                pbi = 6 + nb3 % 2
                if nb3 >= 2:
                    P.wait("tensor", s_ve, D3_ev[nb3 - 2])
                nb3 += 1
                lhsT = AQ[0:112, 64 * q:64 * q + 128]
                P.inc("tensor", s_pe, lambda e, lhsT=lhsT, toff=toff, tn=tn,
                      pbi=pbi: e.matmul(PB[pbi][:, 0:tn], lhsT,
                                        v_tapf[0:112, toff:toff + tn],
                                        start=True, stop=True))
                BMM = s_pe.n
                P.wait("vector", s_pe, BMM)
                if nb3 == 1:
                    P.wait("vector", s_ve, AOZ)
                P.emit("vector", lambda e, c=c, dv=dv, toff=toff, tn=tn, pbi=pbi:
                       e.tensor_tensor(
                           out=tmp2[:, 0:tn], in0=PB[pbi][:, 0:tn],
                           in1=AV[:, LB * c + L0 + dv + toff:
                                  LB * c + L0 + dv + toff + tn],
                           op=OP.mult))
                P.inc("vector", s_ve, lambda e, c=c, toff=toff, tn=tn:
                      e.tensor_tensor(
                          out=AK[:, TOWN * c + toff: TOWN * c + toff + tn],
                          in0=AK[:, TOWN * c + toff: TOWN * c + toff + tn],
                          in1=tmp2[:, 0:tn], op=OP.add))
                D3_ev[nb3 - 1] = s_ve.n
    AO = s_ve.n

    fence()

    if SL < 6:
        _finish(nc, P, block_debug=(d_dbg, locals()))
        return nc
    P.wait("tensor", s_wo, WOUT)
    D4_ev = {}
    nb4 = 0
    for tc in range(NTO):
        for jc in range(2):
            pbi = nb4 % 2
            if nb4 >= 2:
                P.wait("tensor", s_ve, D4_ev[nb4 - 2])
            nb4 += 1

            def smm(e, tc=tc, jc=jc, pbi=pbi):
                for c in range(8):
                    e.matmul(PB[pbi][:, 0:512],
                             AK[:, TOWN * c + 128 * tc: TOWN * c + 128 * (tc + 1)],
                             v_woutb[c][:, 512 * jc:512 * (jc + 1)],
                             start=(c == 0), stop=False)
                return e.matmul(PB[pbi][:, 0:512],
                                countv[0:1, L0 + 128 * tc: L0 + 128 * (tc + 1)],
                                bouts[0:1, 512 * jc:512 * (jc + 1)],
                                start=False, stop=True)
            P.inc("tensor", s_pe, smm)
            SMM = s_pe.n
            P.wait("vector", s_pe, SMM)
            P.inc("vector", s_ve, lambda e, tc=tc, jc=jc, pbi=pbi: e.tensor_copy(
                AQ[:, 1024 * tc + 512 * jc: 1024 * tc + 512 * (jc + 1)],
                PB[pbi][:, 0:512]))
            D4_ev[nb4 - 1] = s_ve.n
    SDONE = s_ve.n

    fence()

    if SL < 7:
        _finish(nc, P, block_debug=(d_dbg, locals()))
        return nc
    E0gt_pe = {}
    nb5 = 0
    E0_b2ev = {}
    for tc in range(NTO):
        pbi = 2 + tc % 2
        if tc == 0:
            P.wait("tensor", s_ve, GATES)
        P.inc("tensor", s_pe, lambda e, tc=tc, pbi=pbi: e.transpose(
            PB[pbi][0:8, 0:128], gw[:, 8 * tc:8 * (tc + 1)], eyef[:]))
        GTP = s_pe.n
        P.wait("vector", s_pe, GTP)
        if tc >= 1:
            P.wait("vector", s_pe, E0gt_pe[tc - 1])
        P.inc("vector", s_ve, lambda e, pbi=pbi: e.tensor_copy(
            gT[:], PB[pbi][0:8, 0:128]))
        GTEV = s_ve.n
        P.wait("tensor", s_ve, GTEV)
        for jc in range(2):
            pbj = nb5 % 2
            if nb5 >= 2:
                P.wait("tensor", s_ve, E0_b2ev[nb5 - 2])
            nb5 += 1
            P.inc("tensor", s_pe, lambda e, jc=jc, pbj=pbj: e.matmul(
                PB[pbj][:, 0:512], gT[:], b2s[:, 512 * jc:512 * (jc + 1)],
                start=True, stop=True))
            B2MM = s_pe.n
            E0gt_pe[tc] = B2MM
            P.wait("vector", s_pe, B2MM)
            P.inc("vector", s_ve, lambda e, tc=tc, jc=jc, pbj=pbj: e.tensor_copy(
                AV[:, 1024 * tc + 512 * jc: 1024 * tc + 512 * (jc + 1)],
                PB[pbj][:, 0:512]))
            E0_b2ev[nb5 - 1] = s_ve.n

    fence()

    if SL < 8:
        _finish(nc, P, block_debug=(d_dbg, locals()))
        return nc
    GACC_HALF_V = [None]
    BANK_EV = [s_ve.n] * 8
    w2_pre = {}
    w1done = []
    w1_pre = {}

    def emit_w2_load(e_, jc, fc, widx):
        if widx >= 4:
            P.wait("gpsimd", s_pe, w2done[widx - 4])
        P.dma("gpsimd", s_w2[widx % 4], v_w2buf[widx % 4],
              d_w2[e_, 128 * fc:128 * (fc + 1), 512 * jc:512 * (jc + 1)])
        return s_w2[widx % 4].n

    def emit_w1_group(e_, fg):
        gidx = 16 * (2 * 0 + 0)
        gi = emit_w1_group.count
        emit_w1_group.count += 1
        if gi >= 2:
            P.wait("gpsimd", s_pe, w1done[gi - 2])
        for dc in range(8):
            P.dma("gpsimd", s_w1[fg % 2], wgrp(fg % 2, dc),
                  d_w1[e_, 128 * dc:128 * (dc + 1), 512 * fg:512 * (fg + 1)])
        return s_w1[fg % 2].n
    emit_w1_group.count = 0
    gelu_seq = []
    w2done = []
    P2_EV_PREV = 0
    PH2_PE_PREV = 0
    for e_ in range(NE):
        for tb in range(2):
            P.wait("scalar", s_pe, PH2_PE_PREV)
            nb0 = len(gelu_seq)
            for fg in range(8):
                if (e_, tb, fg) in w1_pre:
                    W1L = w1_pre[(e_, tb, fg)]
                else:
                    W1L = emit_w1_group(e_, fg)

                P.wait("tensor", s_w1[fg % 2], W1L)
                if fg == 5:
                    for pfc in range(4):
                        w2_pre[(e_, tb, 0, pfc)] = emit_w2_load(
                            e_, 0, pfc, len(w2done) + pfc)
                for fsub in range(4):
                    fc = 4 * fg + fsub
                    for th in range(2):
                        nbank = len(gelu_seq)
                        pbi = nbank % 4
                        if nbank - nb0 < 4:
                            P.wait("tensor", s_ve, BANK_EV[pbi])
                        if nbank >= 4:
                            P.wait("tensor", s_sc, gelu_seq[nbank - 4])

                        def hmm(e, e_=e_, tb=tb, fg=fg, fsub=fsub, th=th, pbi=pbi):
                            for dc in range(8):
                                i = e.matmul(
                                    PB[pbi][:, 0:512],
                                    wgrp(fg % 2, dc)[:, 128 * fsub:128 * (fsub + 1)],
                                    XF[:, LB * dc + L0 + 1024 * tb + 512 * th:
                                       LB * dc + L0 + 1024 * tb + 512 * (th + 1)],
                                    start=(dc == 0), stop=(dc == 7))
                            return i
                        P.inc("tensor", s_pe, hmm)
                        HMM = s_pe.n
                        P.wait("scalar", s_pe, HMM)
                        P.inc("scalar", s_sc, lambda e, e_=e_, fc=fc, th=th,
                              pbi=pbi: e.activation(
                                  out=h_view(fc, 512 * th, 512 * (th + 1)),
                                  in_=PB[pbi][:, 0:512], func=AF.Gelu,
                                  bias=b1s[:, 32 * e_ + fc:32 * e_ + fc + 1]))
                        gelu_seq.append(s_sc.n)
                w1done.append(s_pe.n)
            H_READY = s_sc.n

            for jc in range(2):
                for fc in range(32):
                    widx = len(w2done)
                    if (e_, tb, jc, fc) in w2_pre:
                        W2L = w2_pre[(e_, tb, jc, fc)]
                    else:
                        W2L = emit_w2_load(e_, jc, fc, widx)
                    if jc == 1 and fc in (16, 20) and (e_, tb) != (NE - 1, 1):
                        ne_, ntb = (e_, 1) if tb == 0 else (e_ + 1, 0)
                        nfg = 0 if fc == 16 else 1
                        w1_pre[(ne_, ntb, nfg)] = emit_w1_group(ne_, nfg)
                    P.wait("tensor", s_w2[widx % 4], W2L)
                    if fc == 0:
                        P.wait("tensor", s_sc, H_READY)
                        for tcc in range(8):
                            P.wait("tensor", s_ve, BANK_EV[tcc])

                            def mm0(e, jc=jc, tcc=tcc, widx=widx):
                                return e.matmul(
                                    PB[tcc][:, 0:512],
                                    h_view(0, 128 * tcc, 128 * (tcc + 1)),
                                    v_w2buf[widx % 4][:, 0:512],
                                    start=True, stop=False)
                            if tcc == 7:
                                P.inc("tensor", s_pe, mm0)
                            else:
                                P.emit("tensor", lambda e, f=mm0: f(e))
                        w2done.append(s_pe.n)
                        continue

                    def omm(e, jc=jc, fc=fc, widx=widx):
                        for tcc in range(8):
                            i = e.matmul(
                                PB[tcc][:, 0:512],
                                h_view(fc, 128 * tcc, 128 * (tcc + 1)),
                                v_w2buf[widx % 4][:, 0:512],
                                start=False, stop=(fc == 31))
                        return i
                    P.inc("tensor", s_pe, omm)
                    w2done.append(s_pe.n)
                OMM = s_pe.n
                P.wait("vector", s_pe, OMM)
                for tcc in range(8):
                    tcg = 8 * tb + tcc
                    P.emit("vector", lambda e, e_=e_, tcg=tcg, tcc=tcc:
                           e.tensor_scalar(
                               out=tmp2[:, 0:512], in0=PB[tcc][:, 0:512],
                               scalar1=gw[:, 8 * tcg + e_:8 * tcg + e_ + 1],
                               scalar2=None, op0=OP.mult))
                    P.inc("vector", s_ve, lambda e, tcg=tcg, jc=jc:
                          e.tensor_tensor(
                              out=AV[:, 1024 * tcg + 512 * jc:
                                     1024 * tcg + 512 * (jc + 1)],
                              in0=AV[:, 1024 * tcg + 512 * jc:
                                     1024 * tcg + 512 * (jc + 1)],
                              in1=tmp2[:, 0:512], op=OP.add))
                    BANK_EV[tcc] = s_ve.n
                P2_EV_PREV = s_ve.n
                if (e_, tb, jc) == (NE - 1, 0, 1):
                    GACC_HALF_V[0] = s_ve.n
            PH2_PE_PREV = s_pe.n
    GACC = s_ve.n
    GACC_HALF = GACC

    fence()

    if SL < 9:
        _finish(nc, P, block_debug=(d_dbg, locals()))
        return nc
    F_ev, F_mul = {}, {}
    for tc in range(NTO):
        bb = tc % 2
        xb = tc % 4
        ob = tc % 3
        if tc >= 4:
            P.wait("sync", s_ve, F_ev[tc - 4])
        P.dma("sync", s_x2[xb % 2], v_xt2[xb][:],
              d_xh[L0 + 128 * tc: L0 + 128 * (tc + 1), :])
        X2 = s_x2[xb % 2].n
        if tc == 0:
            P.wait("scalar", s_ve, GACC)
        if tc >= 2:
            P.wait("scalar", s_ve, F_mul[tc - 2])
        P.inc("scalar", s_sc, lambda e, tc=tc, bb=bb: e.activation(
            out=v_sg[bb][:], in_=AV[:, 1024 * tc:1024 * (tc + 1)], func=AF.Silu))
        SGD = s_sc.n
        P.wait("vector", s_sc, SGD)
        P.wait("vector", s_x2[xb % 2], X2)
        if tc == 0:
            P.wait("vector", s_ve, SDONE)
        if tc >= 3:
            P.wait("vector", s_st[ob], s_st[ob].n)
        P.inc("vector", s_ve, lambda e, tc=tc, bb=bb, ob=ob: e.tensor_tensor(
            out=v_outb[ob][:], in0=v_sg[bb][:],
            in1=AQ[:, 1024 * tc:1024 * (tc + 1)], op=OP.mult))
        F_mul[tc] = s_ve.n
        P.inc("vector", s_ve, lambda e, xb=xb, ob=ob: e.tensor_tensor(
            out=v_outb[ob][:], in0=v_outb[ob][:], in1=v_xt2[xb][:], op=OP.add))
        F_ev[tc] = s_ve.n
        P.wait("sync", s_ve, F_ev[tc])
        P.dma("sync", s_st[ob], d_out[128 * tc:128 * (tc + 1), :], v_outb[ob][:])

    if d_dbg:
        ctx = dict(XF=XF, AQ=AQ, AK=AK, AV=AV, H2=H2, WS=WS, glog=glog, gw=gw,
                   v_ddf=v_ddf, v_ddtok=v_ddtok, v_taptok=v_taptok,
                   v_tapf=v_tapf, STAT=STAT, sbc=sbc, H2f=H2f)
        P.wait("sync", s_ve, s_ve.n)
        P.wait("sync", s_sc, s_sc.n)
        for name, ap in d_dbg.items():
            P.dma("sync", s_st[0], ap, DBG_SRC[name](ctx))

    for _s in s_st:
        P.wait("sync", _s, _s.n)

    with nc.allow_low_precision("bf16 accumulators: 4-term softmax/tap sums, "
                                "ao 7-tap sum, g_acc 2-expert sum (tolerance "
                                "validated)"):
        with nc.Block() as block:
            P.run(block)
    return nc


def _host_prep(L0_attn_x, L0_attn_w_qkv, L0_attn_w_out, L0_attn_b_out,
               L0_ffn_centroids, L0_ffn_W1, L0_ffn_b1, L0_ffn_W2, L0_ffn_b2):
    bf = ml_dtypes.bfloat16
    x = np.ascontiguousarray(np.asarray(L0_attn_x, dtype=np.float32))
    wqkv = np.ascontiguousarray(np.asarray(L0_attn_w_qkv, dtype=np.float32))
    wout = np.ascontiguousarray(np.asarray(L0_attn_w_out, dtype=np.float32))
    bout = np.ascontiguousarray(np.asarray(L0_attn_b_out, dtype=np.float32)
                                .reshape(1, D))
    cent = np.asarray(L0_ffn_centroids, dtype=np.float32)
    ctt = np.ascontiguousarray(cent.T)
    w1 = np.ascontiguousarray(np.asarray(L0_ffn_W1, dtype=np.float32))
    b1 = np.asarray(L0_ffn_b1, dtype=np.float32)
    b1t = np.ascontiguousarray(
        b1.reshape(NE, 32, 128).transpose(2, 0, 1).reshape(128, NE * 32))
    w2 = np.ascontiguousarray(np.asarray(L0_ffn_W2, dtype=np.float32))
    b2 = np.ascontiguousarray(np.asarray(L0_ffn_b2, dtype=np.float32))

    eyeb = np.eye(128, dtype=bf)
    eyef = np.eye(128, dtype=np.float32)
    bonesA = np.zeros((128, 256), dtype=np.float32)
    bonesB = np.zeros((128, 256), dtype=np.float32)
    for c in range(8):
        for hh in range(16):
            if hh // 2 == c:
                rows = slice(64 * (hh % 2), 64 * (hh % 2 + 1))
                bonesA[rows, 32 * c + hh] = 1.0
                bonesB[rows, 32 * c + 16 + hh] = 1.0
    bonesA = bonesA.astype(bf)
    bonesB = bonesB.astype(bf)
    eb = np.zeros((128, 7168), dtype=np.float32)
    for kk in range(112):
        eb[kk, 64 * kk:64 * kk + 64] = 1.0
    eb = eb.astype(bf)
    eone = np.zeros((128, 224), dtype=np.float32)
    for kk in range(112):
        eone[kk, 2 * kk] = 1.0
        if 2 * kk + 1 < 224:
            eone[kk, 2 * kk + 1] = 1.0
    eone = eone.astype(bf)

    wmax = (SEQ - 4) // 2
    in_maps = []
    for core in range(NC):
        bb, j = divmod(core, 4)
        t0 = TOWN * j
        lo, hi = t0 - L0, t0 + TOWN + L0
        xs = x[bb, max(0, lo):min(SEQ, hi)]
        xh = np.zeros((LB, D), dtype=np.float32)
        xh[max(0, -lo): LB - max(0, hi - SEQ)] = xs
        tg = t0 + np.arange(-L0, TOWN + L0)
        wlo = np.maximum(np.ceil((tg - 3) / 2.0), 0)
        whi = np.minimum(np.floor(tg / 2.0), wmax)
        cv = np.maximum(whi - wlo + 1, 0).astype(np.float32)
        countv = cv.reshape(1, LB)[:, :2176].astype(bf)
        masks = np.zeros((128, 64), dtype=np.float32)
        for qi in range(4):
            for tc in range(16):
                t = t0 + 128 * tc + np.arange(128)
                ok = (t % 2) == (qi % 2)
                w = (t - qi) // 2
                ok &= (w >= 0) & (w <= wmax)
                masks[:, 16 * qi + tc] = ok.astype(np.float32)
        desc8 = np.tile((8.0 - np.arange(8, dtype=np.float32)), 16)
        desc8 = np.tile(desc8.reshape(1, 128), (128, 1))
        in_maps.append({
            "xh": xh, "wqkv": wqkv, "wout": wout, "bout": bout, "ctt": ctt,
            "desc8": desc8,
            "w1": w1, "b1t": b1t, "w2": w2, "b2": b2,
            "eyeb": eyeb, "eyef": eyef, "bonesA": bonesA, "bonesB": bonesB,
            "eone": eone, "eb": eb,
            "masks": masks, "countv": countv,
        })
    return in_maps


def kernel(L0_attn_x, L0_attn_w_qkv, L0_attn_w_out, L0_attn_b_out,
           L0_ffn_centroids, L0_ffn_W1, L0_ffn_b1, L0_ffn_W2, L0_ffn_b2):
    in_maps = _host_prep(L0_attn_x, L0_attn_w_qkv, L0_attn_w_out, L0_attn_b_out,
                         L0_ffn_centroids, L0_ffn_W1, L0_ffn_b1, L0_ffn_W2,
                         L0_ffn_b2)
    nc = build_program(DBG.items())
    res = run_bass_kernel_spmd(nc, in_maps, core_ids=list(range(NC)))
    out = np.empty((B, SEQ, D), dtype=np.float32)
    for core in range(NC):
        bb, j = divmod(core, 4)
        out[bb, TOWN * j:TOWN * (j + 1)] = res.results[core]["out"]
    if DBG:
        kernel.dbg = [{k: r["dbg_" + k] for k in dict(DBG)}
                      for r in res.results]
    return out
